# version 01
# baseline (speedup 1.0000x reference)
import numpy as np
import ml_dtypes
from contextlib import ExitStack

import concourse.bass as bass
import concourse.mybir as mybir
from concourse.bass_utils import run_bass_kernel_spmd

F32 = mybir.dt.float32
BF16 = mybir.dt.bfloat16
AF = mybir.ActivationFunctionType
ALU = mybir.AluOpType
AX = mybir.AxisListType

D = 1024
IC = 8
S = 8192
HALF = 4096
NT_OWN = 8
NT_FULL = 16
QT = 32
H0 = 4096
HC0 = 32
E = 8
HID = 2048
F1 = 1024
EPS = 1e-5
C1 = 0.044715
S2 = float(2.0 * np.sqrt(2.0 / np.pi))


class Sem:
    def __init__(self, nc, es, name):
        self.h = es.enter_context(nc.semaphore(name))
        self.n = 0

    def inc(self, ins, k=1):
        ins.then_inc(self.h, k)
        self.n += k
        return self.n

    def tok(self):
        return (self.h, self.n)


class DS:

    def __init__(self, B, name, queue, parity=1):
        self.sems = [Sem(B.nc, B.es, f"{name}_{i}") for i in range(parity)]
        self.k = 0
        B.qsems[queue].extend(self.sems)

    def nxt(self):
        s = self.sems[self.k % len(self.sems)]
        self.k += 1
        return s


_SHIM_SKIP = ("wait_ge", "nop", "drain", "dma_start", "dma_start_transpose",
              "then_inc", "sem_inc", "collective_compute")


class Shim:

    def __init__(self, eng, recent):
        self._e = eng
        self._r = recent

    def __getattr__(self, k):
        f = getattr(self._e, k)
        if k.startswith("_") or k in _SHIM_SKIP or not callable(f):
            return f

        def w(*a, **kw):
            names = set()
            for x in list(a[1:]) + list(kw.values()):
                t = getattr(x, "tensor", None)
                if t is not None:
                    names.add(t.name)
            if names & set(self._r):
                self._e.drain()
                self._r.clear()
            r = f(*a, **kw)
            if a:
                t = getattr(a[0], "tensor", None)
                if t is not None:
                    self._r.append(t.name)
                    while len(self._r) > 6:
                        self._r.pop(0)
            return r
        return w


class ShimBlk:
    def __init__(self, blk, B):
        self._blk = blk
        self._vr = []
        self._sr = []

    @property
    def sync(self):
        return self._blk.sync

    @property
    def gpsimd(self):
        return self._blk.gpsimd

    @property
    def tensor(self):
        return self._blk.tensor

    @property
    def vector(self):
        def deco(fn):
            def wrapped(v):
                return fn(Shim(v, self._vr))
            for k, d in zip(fn.__code__.co_varnames[1:], fn.__defaults__ or ()):
                pass
            if fn.__defaults__:
                def wrapped(v, __fn=fn):
                    return __fn(Shim(v, self._vr))
            return self._blk.vector(wrapped)
        return deco

    @property
    def scalar(self):
        def deco(fn):
            if fn.__defaults__:
                def wrapped(sc, __fn=fn):
                    return __fn(Shim(sc, self._sr))
            else:
                def wrapped(sc):
                    return fn(Shim(sc, self._sr))
            return self._blk.scalar(wrapped)
        return deco


def ssync(B, eng, sem, ins):
    sem.inc(ins)
    eng.wait_ge(sem.h, sem.n)


def mm_group(t, psum_ap, pairs):
    n = len(pairs)
    ins = None
    fp32 = False
    for i, (l, r) in enumerate(pairs):
        if l.tensor.dtype == F32 or r.tensor.dtype == F32:
            fp32 = True
        ins = t.matmul(psum_ap, l, r, start=(i == 0), stop=(i == n - 1))
    if fp32:
        ins = t.drain()
    return ins


class Builder:
    def __init__(self, nc, es, mode, dbg=()):
        self.nc = nc
        self.es = es
        self.mode = mode
        self.dbg = list(dbg)
        self.dram = {}
        self.inp = {}
        self.qsems = {"sync": [], "gp": [], "sc": []}
        self.pe = Sem(nc, es, "pe")
        self.ve = Sem(nc, es, "ve")
        self.se = Sem(nc, es, "se")
        self.bar = Sem(nc, es, "bar")
        self.ccs = Sem(nc, es, "ccs")
        self.nbar = 0

    def din(self, name, shape, dt=F32):
        ap = self.nc.dram_tensor(name, shape, dt, kind="ExternalInput").ap()
        self.inp[name] = ap
        return ap

    def dout(self, name, shape, dt=F32):
        return self.nc.dram_tensor(name, shape, dt, kind="ExternalOutput").ap()

    def dint(self, name, shape, dt=F32):
        ap = self.nc.dram_tensor(name, shape, dt, kind="Internal").ap()
        self.dram[name] = (ap, shape, dt)
        return ap

    def d(self, name):
        return self.dram[name][0]

    def sbuf(self, name, shape, dt=F32):
        return self.es.enter_context(self.nc.sbuf_tensor(name, shape, dt))

    def psum(self, name, shape, dt=F32):
        return self.es.enter_context(self.nc.psum_tensor(name, shape, dt))

    def barrier(self, blk):
        self.nbar += 5
        tgt = self.nbar
        b = self.bar

        @blk.sync
        def _(sync):
            for s in self.qsems["sync"]:
                sync.wait_ge(s.h, s.n)
            b.inc(sync.nop())
            sync.wait_ge(b.h, tgt)

        @blk.gpsimd
        def _(gp):
            for s in self.qsems["gp"]:
                gp.wait_ge(s.h, s.n)
            b.inc(gp.nop())
            gp.wait_ge(b.h, tgt)

        @blk.scalar
        def _(sc):
            for s in self.qsems["sc"]:
                sc.wait_ge(s.h, s.n)
            b.inc(sc.drain())
            sc.wait_ge(b.h, tgt)

        @blk.tensor
        def _(t):
            b.inc(t.drain())
            t.wait_ge(b.h, tgt)

        @blk.vector
        def _(v):
            b.inc(v.drain())
            v.wait_ge(b.h, tgt)
        assert b.n == tgt



def build_program(mode, dbg=(), max_ph=99):
    nc = bass.Bass("TRN2", target_bir_lowering=False)
    es = ExitStack()
    with es:
        B = Builder(nc, es, mode, dbg)
        B.max_ph = max_ph
        _declare(B)
        blk = es.enter_context(nc.Block())
        _emit(B, blk)
    return nc


def _declare(B):
    m = B.mode
    B.din("c_ident_bf", [128, 128], BF16)
    B.din("c_ident_f32", [128, 128], F32)
    B.din("c_ones_f32", [1, 128], F32)
    B.din("c_eps", [128, 1], F32)

    if m in ("l0", "fused"):
        B.din("xo", [HALF, D])
        B.din("xp", [HALF, D])
        B.din("W_spin", [D, D]); B.din("b_spin", [D])
        B.din("W_global", [D, D]); B.din("W_J", [D, D]); B.din("beta", [1])
        B.din("r0w", [E, D])
        B.din("w1", [E, H0, D]); B.din("b1", [E, H0])
        B.din("w2", [E, D, H0]); B.din("b2", [E, D])
        B.dint("wsbf", [D, D], BF16)
        B.dint("wjbf", [D, D], BF16)
        B.dint("wgbf", [D, D], BF16)
        B.dint("w1bf", [E, H0, D], BF16)
        B.dint("w2bf", [E, D, H0], BF16)
        B.dint("sfm", [IC, 128, S], BF16)
        B.dint("stm", [S, D], BF16)
        B.dint("e0", [IC, 128, HALF], BF16)
        B.dint("xn0", [IC, 128, HALF], BF16)
        B.dint("acc0", [HALF, D])
        B.dint("sattn", [HALF, D])
        B.dint("hsp", [HC0, 128, HALF], BF16)
        B.dint("g0", [HALF, D])
        B.dint("globr", [1, D])
        B.dint("cw0d", [QT * 128, E])
        B.dint("lg0d", [QT * 128, E])
        B.dint("r0Td", [128, IC * E])
    B.dint("x1own", [HALF, D])
    if m == "l0":
        B.x1o_out = B.dout("x1o", [HALF, D])
    if m == "fused":
        B.dint("x1full", [S, D])
    if m in ("l1", "fused"):
        if m == "l1":
            B.din("x1f_in", [S, D])
            B.din("x1own_in", [HALF, D])
        B.din("ln_scale", [D]); B.din("ln_bias", [D])
        B.din("W_hidden", [D, 2 * HID]); B.din("b_hidden", [2 * HID])
        B.din("W_qk", [D, 128]); B.din("b_qk", [128])
        B.din("os_weight", [2, 128]); B.din("os_bias", [2, 128])
        B.din("W_out", [HID, D]); B.din("b_out", [D])
        B.din("r1w", [D, E])
        B.din("l1w1", [E, D, F1]); B.din("l1w2", [E, F1, D])
        B.din("sgu", [D, 2 * F1]); B.din("sdown", [F1, D]); B.din("sgw", [D, 1])
        B.dint("xn1", [IC, 128, HALF], BF16)
        B.dint("xn2o", [IC, 128, HALF], BF16)
        B.dint("v1d", [S, HID], BF16)
        B.dint("gate1", [HALF, HID], BF16)
        B.dint("moe1", [HALF, D])
        B.dint("acc1", [HALF, HID])
        B.dint("cw1d", [QT * 128, E])
        B.out_own = B.dout("oout", [HALF, D])

    B.dbg_outs = []
    for name in B.dbg:
        ap, shape, dt = B.dram[name]
        o = B.dout("dbg_" + name, shape, dt)
        B.dbg_outs.append((o, ap))


def _emit(B, blk):
    blk = ShimBlk(blk, B)
    m = B.mode
    mp = getattr(B, "max_ph", 99)
    if m in ("l0", "fused"):
        phs = [_ph_startup_l0, _ph_spass, _ph_glob,
               lambda b, k: _ph_attn0(b, k, 0), lambda b, k: _ph_attn0(b, k, 1),
               _ph_moe0, _ph_combine0]
        for i, f in enumerate(phs):
            if i >= mp:
                break
            f(B, blk)
            B.barrier(blk)
        if mp < 99:
            m = "_dbg_stop"
    if m == "fused":
        @blk.gpsimd
        def _(gp):
            gp.collective_compute(
                "AllGather", ALU.bypass,
                replica_groups=[[0, 1], [2, 3], [4, 5], [6, 7]],
                ins=[B.d("x1own")[:]], outs=[B.d("x1full")[:]],
            ).then_inc(B.ccs.h, 1)
    if m in ("l1", "fused"):
        x1own_src = B.d("x1own") if m == "fused" else B.inp["x1own_in"]
        x1full_src = B.d("x1full") if m == "fused" else B.inp["x1f_in"]
        _ph_startup_l1(B, blk)
        B.barrier(blk)
        _ph_ln_own(B, blk, x1own_src)
        B.barrier(blk)
        _ph_moe1(B, blk)
        B.barrier(blk)
        _ph_ln_full(B, blk, x1full_src, wait_cc=(m == "fused"))
        B.barrier(blk)
        _ph_attn1(B, blk, 0)
        B.barrier(blk)
        _ph_attn1(B, blk, 1)
        B.barrier(blk)
        _ph_yproj(B, blk, x1own_src)
        B.barrier(blk)
    if m == "l0":
        fs = DS(B, "f_x1o", "sync")

        @blk.sync
        def _(sync):
            s = fs.nxt()
            s.inc(sync.dma_start(out=B.x1o_out[:], in_=B.d("x1own")[:]), 16)
            sync.wait_ge(s.h, s.n)
    if B.dbg_outs:
        fd = DS(B, "f_dbg", "sync")

        @blk.sync
        def _(sync):
            s = fd.nxt()
            for o, srcap in B.dbg_outs:
                s.inc(sync.dma_start(out=o[:], in_=srcap[:]), 16)
            sync.wait_ge(s.h, s.n)



def _ph_startup_l0(B, blk):
    nc, I = B.nc, B.inp
    ph = ExitStack()
    pps = lambda n, s, d=F32: ph.enter_context(nc.psum_tensor(n, s, d))
    B.ident_bf = B.sbuf("ident_bf", [128, 128], BF16)
    B.ident_f = B.sbuf("ident_f", [128, 128], F32)
    B.ones_f = B.sbuf("ones_f", [1, 128], F32)
    B.r0T = B.sbuf("r0T", [128, IC * E], F32)
    B.bspin_c = B.sbuf("bspin_c", [128, IC], F32)
    B.beta_c = B.sbuf("beta_c", [128, 1], F32)
    B.b1c = B.sbuf("b1c", [128, E * HC0], F32)
    B.cw0 = B.sbuf("cw0", [128, QT * E], F32)
    B.mean0 = B.sbuf("mean0", [128, IC], F32)
    B.mx0 = B.sbuf("mx0", [128, QT], F32)
    B.sum0 = B.sbuf("sum0", [128, QT], F32)
    B.eps_c = B.sbuf("eps_c", [128, 1], F32)
    B.lgd = B.sbuf("lgd", [128, QT * E], F32)

    st_row = ph.enter_context(nc.sbuf_tensor("st_row", [32, 128], F32))
    st_beta = ph.enter_context(nc.sbuf_tensor("st_beta", [1, 1], F32))
    ps_t = pps("su_ps", [128, 512], F32)

    B.g_wsm = DS(B, "g_wsm", "gp")
    B.g_w1c = DS(B, "g_w1c", "gp")
    B.g_w2c = DS(B, "g_w2c", "gp")
    s_stage = DS(B, "s_stage", "sync")
    B.su_stage = s_stage

    @blk.gpsimd
    def _(gp):
        s = B.g_wsm.nxt()
        for src, dst in [(I["W_spin"], B.d("wsbf")), (I["W_J"], B.d("wjbf")),
                         (I["W_global"], B.d("wgbf"))]:
            s.inc(gp.dma_start(out=dst[:], in_=src[:]), 16)
        s = B.g_w1c.nxt()
        for e in range(E):
            s.inc(gp.dma_start(out=B.d("w1bf")[e], in_=I["w1"][e]), 16)
        s = B.g_w2c.nxt()
        for e in range(E):
            s.inc(gp.dma_start(out=B.d("w2bf")[e], in_=I["w2"][e]), 16)
    B.tk_wsm = B.g_wsm.sems[0].tok()
    B.tk_w1c = B.g_w1c.sems[0].tok()
    B.tk_w2c = B.g_w2c.sems[0].tok()

    @blk.sync
    def _(sync):
        s = s_stage.nxt()
        s.inc(sync.dma_start(out=B.ident_bf[:], in_=I["c_ident_bf"][:]), 16)
        s.inc(sync.dma_start(out=B.ident_f[:], in_=I["c_ident_f32"][:]), 16)
        s.inc(sync.dma_start(out=B.ones_f[:], in_=I["c_ones_f32"][:]), 16)
        s.inc(sync.dma_start(out=B.eps_c[:], in_=I["c_eps"][:]), 16)
        s.inc(sync.dma_start(out=st_beta[:], in_=I["beta"][:].rearrange("(a o) -> a o", a=1)), 16)
    ds_consts = s_stage.sems[0].tok()


    @blk.sync
    def _(sync):
        s = s_stage.nxt()
        s.inc(sync.dma_start(out=st_row[0:8, :], in_=I["b_spin"][:].rearrange("(c p) -> c p", p=128)), 16)
    ds1 = s_stage.sems[0].tok()

    @blk.tensor
    def _(t):
        t.wait_ge(*ds1)
        t.transpose(ps_t[:, 0:8], st_row[0:8, :], B.ident_f[0:8, 0:8])
        t.matmul(ps_t[:, 8:9], B.ones_f[:], st_beta[:], start=True, stop=True)
        B.pe.inc(t.drain())
    pe1 = B.pe.n

    @blk.vector
    def _(v):
        v.wait_ge(B.pe.h, pe1)
        v.tensor_copy(B.bspin_c[:], ps_t[:, 0:8])
        _li = v.tensor_copy(B.beta_c[:], ps_t[:, 8:9])
        B.ve.inc(_li)
    ve1 = B.ve.n

    for ic in range(IC):
        @blk.sync
        def _(sync, ic=ic):
            sync.wait_ge(B.ve.h, ve1 + ic)
            s = s_stage.nxt()
            s.inc(sync.dma_start(out=st_row[0:8, :], in_=I["r0w"][:, ic * 128:(ic + 1) * 128]), 16)
        ds_t = s_stage.sems[0].tok()

        @blk.tensor
        def _(t, ds_t=ds_t, ic=ic):
            t.wait_ge(*ds_t)
            t.wait_ge(B.ve.h, ve1 + ic)
            t.transpose(ps_t[:, 0:8], st_row[0:8, :], B.ident_f[0:8, 0:8])
            B.pe.inc(t.drain())
        pe_t = B.pe.n

        @blk.vector
        def _(v, pe_t=pe_t, ic=ic):
            v.wait_ge(B.pe.h, pe_t)
            _li = v.tensor_copy(B.r0T[:, ic * E:(ic + 1) * E], ps_t[:, 0:8])
            B.ve.inc(_li)
    ve_r = B.ve.n

    for e in range(E):
        @blk.sync
        def _(sync, e=e):
            sync.wait_ge(B.ve.h, ve_r + e)
            s = s_stage.nxt()
            s.inc(sync.dma_start(out=st_row[:], in_=I["b1"][e].rearrange("(c p) -> c p", p=128)), 16)
        ds_t = s_stage.sems[0].tok()

        @blk.tensor
        def _(t, ds_t=ds_t, e=e):
            t.wait_ge(*ds_t)
            t.wait_ge(B.ve.h, ve_r + e)
            t.transpose(ps_t[:, 0:32], st_row[:], B.ident_f[0:32, 0:32])
            B.pe.inc(t.drain())
        pe_t = B.pe.n

        @blk.vector
        def _(v, pe_t=pe_t, e=e):
            v.wait_ge(B.pe.h, pe_t)
            _li = v.tensor_copy(B.b1c[:, e * HC0:(e + 1) * HC0], ps_t[:, 0:32])
            B.ve.inc(_li)
    ph.close()



def _ph_spass(B, blk):
    nc, I = B.nc, B.inp
    with ExitStack() as ph:
        sb = lambda n, s, d=F32: ph.enter_context(nc.sbuf_tensor(n, s, d))
        pps = lambda n, s, d=F32: ph.enter_context(nc.psum_tensor(n, s, d))
        wspT = sb("sp_wspT", [128, IC * D], BF16)
        xin = [sb(f"sp_x{i}", [128, 4 * D]) for i in range(2)]
        xnf = sb("sp_xnf", [128, 4 * D])
        sq = sb("sp_sq", [128, D])
        cols = sb("sp_cols", [128, 16])
        xnfm_bf = sb("sp_xnfm_bf", [128, IC * 512], BF16)
        xnfm_f = sb("sp_xnfm_f", [128, IC * 512])
        sfm_st = sb("sp_sfm", [128, IC * 512], BF16)
        stm_st = sb("sp_stm", [128, 4 * D], BF16)
        rcols = sb("sp_rcols", [128, 8])
        ps_a = pps("sp_psa", [128, 512], F32)
        ps_b = pps("sp_psb", [128, 512], F32)
        ps_s = [pps(f"sp_pss{i}", [128, 512], F32) for i in range(2)]
        ps_t = [pps(f"sp_pst{i}", [128, 1024], BF16) for i in range(2)]
        ps_r = pps("sp_psr", [128, 512], F32)

        c_wsp = DS(B, "c_wsp", "sc")

        @blk.scalar
        def _(sc):
            sc.wait_ge(*B.tk_wsm)
            s = c_wsp.nxt()
            for ic in range(IC):
                s.inc(sc.dma_start(out=wspT[:, ic * D:(ic + 1) * D],
                                   in_=B.d("wsbf")[:, ic * 128:(ic + 1) * 128], transpose=True), 16)
        dc_wspT = c_wsp.sems[0].tok()

        sp_x = DS(B, "sp_x", "sync", parity=2)
        sp_o = DS(B, "sp_o", "sync")
        ds_x = []
        ds_out = [None]
        pe_xnT = [0]
        pe_mm_end = [0]
        trh_pe, trh_ve = [], []
        tth_pe, tth_ve = [], []
        sev_ve = []

        for t in range(NT_FULL):
            own = t < NT_OWN
            src = I["xo"] if own else I["xp"]
            r0 = (t % NT_OWN) * 512
            p = t % 2

            @blk.sync
            def _(sync, t=t, p=p, src=src, r0=r0):
                if t >= 2:
                    sync.wait_ge(B.pe.h, pe_mm_end[t - 1])
                s = sp_x.nxt()
                s.inc(sync.dma_start(out=xin[p][:].rearrange("p (s d) -> p s d", s=4),
                                     in_=src[r0:r0 + 512, :].rearrange("(s p) d -> p s d", p=128)), 16)
            ds_x.append(sp_x.sems[t % 2].tok())

            for sub in range(4):
                xv = xin[p][:, sub * D:(sub + 1) * D]
                xo_ = xnf[:, sub * D:(sub + 1) * D]
                c_neg = cols[:, 4 + sub:5 + sub]
                c_var = cols[:, 8 + sub:9 + sub]
                c_rst = cols[:, 12 + sub:13 + sub]

                @blk.vector
                def _(v, t=t, sub=sub, xv=xv, xo_=xo_, c_neg=c_neg, c_var=c_var):
                    if sub == 0:
                        v.wait_ge(*ds_x[t])
                        if t > 0:
                            v.wait_ge(B.pe.h, pe_xnT[t])
                    ssync(B, v, B.ve, v.reduce_sum(cols[:, sub:sub + 1], xv, axis=AX.X))
                    ssync(B, v, B.ve, v.tensor_scalar_mul(c_neg, cols[:, sub:sub + 1], -1.0 / D))
                    v.tensor_scalar_add(xo_, xv, c_neg)
                    v.tensor_mul(sq[:], xo_, xo_)
                    _li = v.reduce_sum(c_var, sq[:], axis=AX.X)
                    B.ve.inc(_li)
                ve_ln = B.ve.n

                @blk.scalar
                def _(sc, ve_ln=ve_ln, c_var=c_var, c_rst=c_rst):
                    sc.wait_ge(B.ve.h, ve_ln)
                    _li = sc.activation(c_rst, c_var, AF.Sqrt, bias=B.eps_c[:, 0:1], scale=1.0 / D)
                    B.se.inc(_li)
                se_t = B.se.n

                @blk.vector
                def _(v, se_t=se_t, c_rst=c_rst, xo_=xo_):
                    v.wait_ge(B.se.h, se_t)
                    ssync(B, v, B.ve, v.reciprocal(c_rst, c_rst))
                    _li = v.tensor_scalar_mul(xo_, xo_, c_rst)
                    B.ve.inc(_li)
                ve_xn = B.ve.n

                for ic in range(IC):
                    pp = ps_a if ic % 2 == 0 else ps_b

                    @blk.tensor
                    def _(tt, ic=ic, ve_xn=ve_xn, xo_=xo_, pp=pp):
                        tt.wait_ge(B.ve.h, ve_xn)
                        if len(trh_pe) >= 2:
                            tt.wait_ge(B.ve.h, trh_ve[-2])
                        _li = tt.transpose(pp[:, 0:128], xo_[:, ic * 128:(ic + 1) * 128],
                                     B.ident_f[:])
                        B.pe.inc(_li)
                    trh_pe.append(B.pe.n)

                    @blk.vector
                    def _(v, ic=ic, sub=sub, own=own, pp=pp, t=t):
                        v.wait_ge(B.pe.h, trh_pe[-1])
                        if t > 0 and len(trh_ve) % 32 == 0:
                            v.wait_ge(B.pe.h, pe_mm_end[t])
                            v.wait_ge(*ds_out[t])
                        dst = xnfm_bf[:, ic * 512 + sub * 128: ic * 512 + sub * 128 + 128]
                        _li = v.tensor_copy(dst, pp[:, 0:128])
                        if own:
                            _li = v.tensor_copy(
                                xnfm_f[:, ic * 512 + sub * 128: ic * 512 + sub * 128 + 128],
                                pp[:, 0:128])
                        B.ve.inc(_li)
                    trh_ve.append(B.ve.n)
            pe_xnT.append(B.pe.n)
            ve_xn_all = B.ve.n

            for oc in range(IC):
                pp = ps_s[oc % 2]

                @blk.tensor
                def _(tt, oc=oc, pp=pp, ve_xn_all=ve_xn_all, dc_wspT=dc_wspT):
                    tt.wait_ge(B.ve.h, ve_xn_all)
                    if oc == 0:
                        tt.wait_ge(*dc_wspT)
                    if len(sev_ve) >= 2:
                        tt.wait_ge(B.ve.h, sev_ve[-2])
                    ins = mm_group(tt, pp[:],
                                   [(wspT[:, ic * D + oc * 128: ic * D + oc * 128 + 128],
                                     xnfm_bf[:, ic * 512:(ic + 1) * 512]) for ic in range(IC)])
                    B.pe.inc(ins)
                pe_s = B.pe.n

                @blk.vector
                def _(v, oc=oc, pp=pp, pe_s=pe_s, t=t):
                    v.wait_ge(B.pe.h, pe_s)
                    if oc == 0 and t > 0:
                        v.wait_ge(*ds_out[t])
                    v.tensor_scalar(sfm_st[:, oc * 512:(oc + 1) * 512], pp[:],
                                    B.bspin_c[:, oc:oc + 1], None, ALU.add)
                    ssync(B, v, B.ve, v.reduce_sum(cols[:, 0:1],
                                                   sfm_st[:, oc * 512:(oc + 1) * 512],
                                                   axis=AX.X))
                    if t > 0:
                        _li = v.tensor_add(B.mean0[:, oc:oc + 1], B.mean0[:, oc:oc + 1],
                                           cols[:, 0:1])
                    else:
                        _li = v.tensor_copy(B.mean0[:, oc:oc + 1], cols[:, 0:1])
                    B.ve.inc(_li)
                sev_ve.append(B.ve.n)
            ve_s_all = B.ve.n

            for oc in range(IC):
                for sub in range(4):
                    k = oc * 4 + sub
                    pp = ps_t[k % 2]

                    @blk.tensor
                    def _(tt, oc=oc, sub=sub, pp=pp, ve_s_all=ve_s_all):
                        tt.wait_ge(B.ve.h, ve_s_all)
                        if len(tth_pe) >= 2:
                            tt.wait_ge(B.ve.h, tth_ve[-2])
                        _li = tt.transpose(pp[:, 0:128],
                                     sfm_st[:, oc * 512 + sub * 128: oc * 512 + sub * 128 + 128],
                                     B.ident_bf[:])
                        B.pe.inc(_li)
                    tth_pe.append(B.pe.n)

                    @blk.vector
                    def _(v, oc=oc, sub=sub, pp=pp, t=t):
                        v.wait_ge(B.pe.h, tth_pe[-1])
                        if len(tth_ve) % 32 == 0 and t > 0:
                            v.wait_ge(*ds_out[t])
                        _li = v.tensor_copy(stm_st[:, sub * D + oc * 128: sub * D + oc * 128 + 128],
                                      pp[:, 0:128])
                        B.ve.inc(_li)
                    tth_ve.append(B.ve.n)

            if own:
                for sub in range(4):
                    @blk.tensor
                    def _(tt, sub=sub):
                        tt.wait_ge(B.ve.h, trh_ve[-1])
                        if sub > 0 or t > 0:
                            tt.wait_ge(B.ve.h, B.ve.n)
                        ins = mm_group(tt, ps_r[:, 0:E],
                                       [(xnfm_f[:, ic * 512 + sub * 128: ic * 512 + sub * 128 + 128],
                                         B.r0T[:, ic * E:(ic + 1) * E]) for ic in range(IC)])
                        B.pe.inc(ins)
                    pe_r = B.pe.n

                    @blk.vector
                    def _(v, pe_r=pe_r):
                        v.wait_ge(B.pe.h, pe_r)
                        lg = rcols[:, 0:E]
                        m1 = cols[:, 0:1]; m2 = cols[:, 1:2]; dd = cols[:, 2:3]
                        v.tensor_copy(lg, ps_r[:, 0:E])
                        qt_ = t * 4 + sub
                        v.tensor_copy(B.lgd[:, qt_ * E:(qt_ + 1) * E], lg)
                        ssync(B, v, B.ve, v.reduce_max(m1, lg, axis=AX.X))
                        v.tensor_scalar(sq[:, 0:E], lg, m1, None, ALU.is_ge)
                        v.scalar_tensor_tensor(sq[:, E:2 * E], sq[:, 0:E], -1e30, lg,
                                               ALU.mult, ALU.add)
                        ssync(B, v, B.ve, v.reduce_max(m2, sq[:, E:2 * E], axis=AX.X))
                        v.tensor_scalar(sq[:, 2 * E:3 * E], sq[:, E:2 * E], m2, None,
                                        ALU.is_ge)
                        _li = v.tensor_sub(dd, m2, m1)
                        B.ve.inc(_li)
                    ve_rr = B.ve.n

                    @blk.scalar
                    def _(sc, ve_rr=ve_rr):
                        sc.wait_ge(B.ve.h, ve_rr)
                        _li = sc.activation(cols[:, 3:4], cols[:, 2:3], AF.Exp)
                        B.se.inc(_li)
                    se_rr = B.se.n

                    @blk.vector
                    def _(v, t=t, sub=sub, se_rr=se_rr):
                        v.wait_ge(B.se.h, se_rr)
                        ee = cols[:, 3:4]
                        den = cols[:, 4:5]; w1n = cols[:, 5:6]; w2n = cols[:, 6:7]
                        ssync(B, v, B.ve, v.tensor_scalar_add(den, ee, 1.0))
                        ssync(B, v, B.ve, v.reciprocal(w1n, den))
                        ssync(B, v, B.ve, v.tensor_mul(w2n, ee, w1n))
                        qtile = t * 4 + sub
                        cw = B.cw0[:, qtile * E:(qtile + 1) * E]
                        v.tensor_scalar(sq[:, 3 * E:4 * E], sq[:, 2 * E:3 * E], w2n,
                                        None, ALU.mult)
                        _li = v.scalar_tensor_tensor(cw, sq[:, 0:E], w1n, sq[:, 3 * E:4 * E],
                                               ALU.mult, ALU.add)
                        B.ve.inc(_li)
            pe_mm_end.append(B.pe.n)
            ve_tile = B.ve.n

            @blk.sync
            def _(sync, t=t, own=own, ve_tile=ve_tile):
                sync.wait_ge(B.ve.h, ve_tile)
                s = sp_o.nxt()
                s.inc(sync.dma_start(
                    out=B.d("sfm").rearrange("o p c -> p o c")[:, :, t * 512:(t + 1) * 512],
                    in_=sfm_st[:].rearrange("p (o c) -> p o c", o=IC)), 16)
                s.inc(sync.dma_start(
                    out=B.d("stm")[t * 512:(t + 1) * 512, :].rearrange("(s p) d -> p s d", p=128),
                    in_=stm_st[:].rearrange("p (s d) -> p s d", s=4)), 16)
                if own:
                    s.inc(sync.dma_start(
                        out=B.d("xn0").rearrange("o p c -> p o c")[:, :, t * 512:(t + 1) * 512],
                        in_=xnfm_bf[:].rearrange("p (o c) -> p o c", o=IC)), 16)
            ds_out.append(sp_o.sems[0].tok())

        @blk.sync
        def _(sync):
            sync.wait_ge(B.ve.h, B.ve.n)
            s = sp_o.nxt()
            s.inc(sync.dma_start(out=B.d("cw0d")[:].rearrange("(q p) e -> p q e", p=128),
                                 in_=B.cw0[:].rearrange("p (q e) -> p q e", e=E)), 16)
            s.inc(sync.dma_start(out=B.d("lg0d")[:].rearrange("(q p) e -> p q e", p=128),
                                 in_=B.lgd[:].rearrange("p (q e) -> p q e", e=E)), 16)
            s.inc(sync.dma_start(out=B.d("r0Td")[:], in_=B.r0T[:]), 16)



def _ph_glob(B, blk):
    nc = B.nc
    with ExitStack() as ph:
        sb = lambda n, s, d=F32: ph.enter_context(nc.sbuf_tensor(n, s, d))
        pps = lambda n, s, d=F32: ph.enter_context(nc.psum_tensor(n, s, d))
        wgT = sb("gl_wgT", [128, IC * D], BF16)
        meanb = sb("gl_meanb", [128, IC], BF16)
        gcol = sb("gl_gcol", [128, IC])
        grow = sb("gl_grow", [8, 128])
        growr = sb("gl_growr", [1, D])
        psA = pps("gl_psA", [128, 512], F32)
        psB = pps("gl_psB", [128, 512], F32)

        c_wg = DS(B, "c_wg", "sc")

        @blk.scalar
        def _(sc):
            sc.wait_ge(*B.tk_wsm)
            s = c_wg.nxt()
            for ic in range(IC):
                s.inc(sc.dma_start(out=wgT[:, ic * D:(ic + 1) * D],
                                   in_=B.d("wgbf")[:, ic * 128:(ic + 1) * 128], transpose=True), 16)
        dc_w = c_wg.sems[0].tok()

        @blk.vector
        def _(v):
            _li = v.tensor_scalar_mul(meanb[:], B.mean0[:], 1.0 / S)
            B.ve.inc(_li)
        ve_m = B.ve.n

        @blk.tensor
        def _(t):
            t.wait_ge(*dc_w)
            t.wait_ge(B.ve.h, ve_m)
            ins = None
            for oc in range(IC):
                ins = mm_group(t, psA[:, oc:oc + 1],
                               [(wgT[:, ic * D + oc * 128: ic * D + oc * 128 + 128],
                                 meanb[:, ic:ic + 1]) for ic in range(IC)])
            B.pe.inc(ins)
        pe_g = B.pe.n

        @blk.vector
        def _(v):
            v.wait_ge(B.pe.h, pe_g)
            _li = v.tensor_copy(gcol[:], psA[:, 0:IC])
            B.ve.inc(_li)
        ve_g = B.ve.n

        @blk.tensor
        def _(t):
            t.wait_ge(B.ve.h, ve_g)
            t.transpose(psB[0:8, 0:128], gcol[:], B.ident_f[:])
            B.pe.inc(t.drain())
        pe_t2 = B.pe.n

        @blk.vector
        def _(v):
            v.wait_ge(B.pe.h, pe_t2)
            _li = v.tensor_copy(grow[:], psB[0:8, 0:128])
            B.ve.inc(_li)
        ve_t2 = B.ve.n

        g_go = DS(B, "g_go", "sync")

        @blk.sync
        def _(sync):
            sync.wait_ge(B.ve.h, ve_t2)
            s = g_go.nxt()
            s.inc(sync.dma_start(out=B.d("globr")[:].rearrange("o (c p) -> (o c) p", p=128),
                                 in_=grow[:]), 16)



def _ph_attn0(B, blk, phz):
    nc = B.nc
    mbase = phz * HALF
    with ExitStack() as ph:
        sb = lambda n, s, d=F32: ph.enter_context(nc.sbuf_tensor(n, s, d))
        pps = lambda n, s, d=F32: ph.enter_context(nc.psum_tensor(n, s, d))
        sfm_h = sb(f"a0_sfm{phz}", [128, IC * HALF], BF16)
        stm_h = sb(f"a0_stm{phz}", [128, 32 * D], BF16)
        if phz == 0:
            wjT = sb("a0_wjT", [128, IC * D], BF16)
            erow = [sb(f"a0_e{phz}_0", [128, IC * 128], BF16)]
        else:
            erow = [sb(f"a0_e{phz}_{i}", [128, IC * 128], BF16) for i in range(2)]
        srow = sb(f"a0_srow{phz}", [128, HALF])
        prob = sb(f"a0_prob{phz}", [128, HALF], BF16)
        probT = sb(f"a0_probT{phz}", [128, HALF], BF16)
        ncol = sb(f"a0_ncol{phz}", [128, 8])
        accw = [sb(f"a0_accw{phz}{i}", [128, D]) for i in range(2)]
        if phz == 1:
            acca = [sb(f"a0_acca{i}", [128, D]) for i in range(2)]
            sown = [sb(f"a0_sown{i}", [128, D], BF16) for i in range(2)]
            sowf = sb("a0_sowf", [128, D])
            glob_bc = sb("a0_glob", [128, D])
            growr = sb("a0_growr", [1, D])
        ps_sc = [pps(f"a0_psc{phz}{i}", [128, 512], F32) for i in range(2)]
        ps_av = [pps(f"a0_pav{phz}{i}", [128, 512], F32) for i in range(2)]
        ps_e = pps(f"a0_pse{phz}", [128, 512], F32)
        ps_t = [pps(f"a0_pst{phz}{i}", [128, 1024], BF16) for i in range(2)]

        a_kv = DS(B, f"a_kv{phz}", "sync")

        @blk.sync
        def _(sync):
            s = a_kv.nxt()
            s.inc(sync.dma_start(out=sfm_h[:].rearrange("p (o c) -> p o c", o=IC),
                                 in_=B.d("sfm").rearrange("o p c -> p o c")[:, :, mbase:mbase + HALF]), 16)
            s.inc(sync.dma_start(out=stm_h[:].rearrange("p (s d) -> p s d", s=32),
                                 in_=B.d("stm")[mbase:mbase + HALF, :].rearrange("(s p) d -> p s d", p=128)), 16)
        ds_kv = a_kv.sems[0].tok()

        if phz == 0:
            c_wj = DS(B, "c_wj", "sc")

            @blk.scalar
            def _(sc):
                sc.wait_ge(*B.tk_wsm)
                s = c_wj.nxt()
                for ic in range(IC):
                    s.inc(sc.dma_start(out=wjT[:, ic * D:(ic + 1) * D],
                                       in_=B.d("wjbf")[:, ic * 128:(ic + 1) * 128], transpose=True), 16)
            dc_wjT = c_wj.sems[0].tok()

        if phz == 1:
            a_gr = DS(B, "a_gr", "sync")

            @blk.sync
            def _(sync):
                s = a_gr.nxt()
                s.inc(sync.dma_start(out=growr[:], in_=B.d("globr")[:]), 16)
            ds_gr = a_gr.sems[0].tok()

            @blk.tensor
            def _(t):
                t.wait_ge(*ds_gr)
                t.matmul(ps_sc[0][:], B.ones_f[:], growr[:, 0:512], start=True, stop=True)
                t.matmul(ps_sc[1][:], B.ones_f[:], growr[:, 512:1024], start=True, stop=True)
                B.pe.inc(t.drain())
            pe_gb = B.pe.n

            @blk.vector
            def _(v):
                v.wait_ge(B.pe.h, pe_gb)
                v.tensor_copy(glob_bc[:, 0:512], ps_sc[0][:])
                _li = v.tensor_copy(glob_bc[:, 512:1024], ps_sc[1][:])
                B.ve.inc(_li)
            ve_gb = B.ve.n

            @blk.tensor
            def _(t):
                t.wait_ge(B.ve.h, ve_gb)
                t.nop()

        if phz == 0:
            a_eo = DS(B, "a_eo", "sync")
            a_ao = DS(B, "a_ao", "sync")
        else:
            a_ei = DS(B, "a_ei", "sync", parity=2)
            a_ai = DS(B, "a_ai", "sync", parity=2)
            a_so = DS(B, "a_so", "sync", parity=2)

        pe_sc_q = [0]
        pe_tr_q = [0]
        pe_av_q = [0]
        se_exp_q = [0]
        ve_acc_q = [0]
        ve_eh = []
        ds_e = []
        ds_acc = [0]

        for q in range(QT):
            eb = erow[q % len(erow)]
            if phz == 0:
                for oc in range(IC):
                    @blk.tensor
                    def _(t, q=q, oc=oc, eb=eb):
                        if q == 0 and oc == 0:
                            t.wait_ge(*ds_kv)
                            t.wait_ge(*dc_wjT)
                        if ve_eh:
                            t.wait_ge(B.ve.h, ve_eh[-1])
                        ins = mm_group(t, ps_e[:, 0:128],
                                       [(wjT[:, ic * D + oc * 128: ic * D + oc * 128 + 128],
                                         sfm_h[:, ic * HALF + q * 128: ic * HALF + q * 128 + 128])
                                        for ic in range(IC)])
                        B.pe.inc(ins)
                    pe_e = B.pe.n

                    @blk.vector
                    def _(v, oc=oc, eb=eb, pe_e=pe_e):
                        v.wait_ge(B.pe.h, pe_e)
                        _li = v.tensor_copy(eb[:, oc * 128:(oc + 1) * 128], ps_e[:, 0:128])
                        B.ve.inc(_li)
                    ve_eh.append(B.ve.n)
                ve_e = B.ve.n

                @blk.sync
                def _(sync, q=q, eb=eb, ve_e=ve_e):
                    sync.wait_ge(B.ve.h, ve_e)
                    s = a_eo.nxt()
                    s.inc(sync.dma_start(
                        out=B.d("e0").rearrange("o p c -> p o c")[:, :, q * 128:(q + 1) * 128],
                        in_=eb[:].rearrange("p (o c) -> p o c", o=IC)), 16)
                e_ready = ("ve", ve_e)
            else:
                @blk.sync
                def _(sync, q=q, eb=eb):
                    if q >= 2:
                        sync.wait_ge(B.pe.h, pe_sc_q[q - 1])
                    s = a_ei.nxt()
                    s.inc(sync.dma_start(
                        out=eb[:].rearrange("p (o c) -> p o c", o=IC),
                        in_=B.d("e0").rearrange("o p c -> p o c")[:, :, q * 128:(q + 1) * 128]), 16)
                e_ready = ("tok", a_ei.sems[q % 2].tok())

            for mc in range(8):
                @blk.tensor
                def _(t, q=q, mc=mc, eb=eb, e_ready=e_ready):
                    if mc == 0:
                        if e_ready[0] == "ve":
                            t.wait_ge(B.ve.h, e_ready[1])
                        else:
                            t.wait_ge(*e_ready[1])
                            if q == 0:
                                t.wait_ge(*ds_kv)
                        t.wait_ge(B.se.h, se_exp_q[q])
                    ins = mm_group(t, ps_sc[mc % 2][:],
                                   [(eb[:, oc * 128:(oc + 1) * 128],
                                     sfm_h[:, oc * HALF + mc * 512: oc * HALF + mc * 512 + 512])
                                    for oc in range(IC)])
                    B.pe.inc(ins)
                pe_mc = B.pe.n

                @blk.vector
                def _(v, q=q, mc=mc, pe_mc=pe_mc):
                    v.wait_ge(B.pe.h, pe_mc)
                    if mc == 0:
                        v.wait_ge(B.se.h, se_exp_q[q])
                    _li = v.tensor_scalar(srow[:, mc * 512:(mc + 1) * 512], ps_sc[mc % 2][:],
                                    1.0 / 32.0, None, ALU.mult)
                    B.ve.inc(_li)
            pe_sc_q.append(B.pe.n)

            @blk.vector
            def _(v, q=q):
                _li = v.reduce_max(ncol[:, 0:1], srow[:], axis=AX.X, negate=True)
                if phz == 0:
                    ssync(B, v, B.ve, _li)
                    _li = v.tensor_copy(B.mx0[:, q:q + 1], ncol[:, 0:1])
                B.ve.inc(_li)
            ve_mx = B.ve.n

            @blk.scalar
            def _(sc, q=q, ve_mx=ve_mx):
                sc.wait_ge(B.ve.h, ve_mx)
                sc.wait_ge(B.pe.h, pe_tr_q[q])
                _li = sc.activation(prob[:], srow[:], AF.Exp, bias=ncol[:, 0:1],
                              accum_out=ncol[:, 1:2])
                B.se.inc(_li)
            se_exp_q.append(B.se.n)
            se_exp = B.se.n

            @blk.vector
            def _(v, q=q, se_exp=se_exp):
                if phz == 0:
                    v.wait_ge(B.se.h, se_exp)
                    _li = v.tensor_copy(B.sum0[:, q:q + 1], ncol[:, 1:2])
                    B.ve.inc(_li)
            ve_sum = B.ve.n

            tr_ve = []
            for mc in range(32):
                @blk.tensor
                def _(t, q=q, mc=mc, se_exp=se_exp):
                    if mc == 0:
                        t.wait_ge(B.se.h, se_exp)
                        t.wait_ge(B.ve.h, ve_acc_q[q])
                    if len(tr_ve) >= 2:
                        t.wait_ge(B.ve.h, tr_ve[-2])
                    _li = t.transpose(ps_t[mc % 2][:, 0:128], prob[:, mc * 128:(mc + 1) * 128],
                                B.ident_bf[:])
                    B.pe.inc(_li)
                pe_tr = B.pe.n

                @blk.vector
                def _(v, mc=mc, pe_tr=pe_tr, q=q):
                    v.wait_ge(B.pe.h, pe_tr)
                    if mc == 0:
                        v.wait_ge(B.pe.h, pe_av_q[q])
                    _li = v.tensor_copy(probT[:, mc * 128:(mc + 1) * 128], ps_t[mc % 2][:, 0:128])
                    B.ve.inc(_li)
                tr_ve.append(B.ve.n)
            pe_tr_q.append(B.pe.n)
            ve_tr = B.ve.n

            @blk.tensor
            def _(t, q=q, ve_tr=ve_tr):
                t.wait_ge(B.ve.h, ve_tr)
                t.wait_ge(B.ve.h, ve_acc_q[q])
                for mc in range(32):
                    for oh in range(2):
                        _li = t.matmul(ps_av[oh][:],
                                 probT[:, mc * 128:(mc + 1) * 128],
                                 stm_h[:, mc * D + oh * 512: mc * D + oh * 512 + 512],
                                 start=(mc == 0), stop=(mc == 31))
                B.pe.inc(_li)
            pe_av_q.append(B.pe.n)
            pe_av = B.pe.n

            aw = accw[q % 2]
            if phz == 0:
                @blk.vector
                def _(v, q=q, aw=aw, pe_av=pe_av):
                    v.wait_ge(B.pe.h, pe_av)
                    if q >= 1:
                        v.wait_ge(*ds_acc[q])
                    v.tensor_copy(aw[:, 0:512], ps_av[0][:])
                    _li = v.tensor_copy(aw[:, 512:1024], ps_av[1][:])
                    B.ve.inc(_li)
                ve_acc_q.append(B.ve.n)
                ve_acc = B.ve.n

                @blk.sync
                def _(sync, q=q, aw=aw, ve_acc=ve_acc):
                    sync.wait_ge(B.ve.h, ve_acc)
                    s = a_ao.nxt()
                    s.inc(sync.dma_start(
                        out=B.d("acc0")[q * 128:(q + 1) * 128, :], in_=aw[:]), 16)
                ds_acc.append(a_ao.sems[0].tok())
            else:
                ab = acca[q % 2]
                so = sown[q % 2]

                @blk.sync
                def _(sync, q=q, ab=ab, so=so):
                    if q >= 2:
                        sync.wait_ge(B.ve.h, ve_acc_q[q - 1])
                    s = a_ai.nxt()
                    s.inc(sync.dma_start(out=ab[:], in_=B.d("acc0")[q * 128:(q + 1) * 128, :]), 16)
                    s.inc(sync.dma_start(out=so[:].rearrange("p (a d) -> p a d", a=1),
                                         in_=B.d("stm")[q * 128:(q + 1) * 128, :]
                                         .rearrange("(s p) d -> p s d", p=128)), 16)
                ds_ld = a_ai.sems[q % 2].tok()

                @blk.vector
                def _(v, q=q, ab=ab, pe_av=pe_av, ds_ld=ds_ld):
                    v.wait_ge(B.pe.h, pe_av)
                    v.wait_ge(*ds_ld)
                    v.tensor_copy(ncol[:, 2:3], B.mx0[:, q:q + 1])
                    v.tensor_copy(ncol[:, 3:4], B.sum0[:, q:q + 1])
                    v.tensor_tensor(ncol[:, 4:5], ncol[:, 0:1], ncol[:, 2:3], ALU.min)
                    v.tensor_sub(ncol[:, 5:6], ncol[:, 4:5], ncol[:, 2:3])
                    _li = v.tensor_sub(ncol[:, 6:7], ncol[:, 4:5], ncol[:, 0:1])
                    B.ve.inc(_li)
                ve_m1 = B.ve.n

                @blk.scalar
                def _(sc, ve_m1=ve_m1):
                    sc.wait_ge(B.ve.h, ve_m1)
                    _li = sc.activation(ncol[:, 5:7], ncol[:, 5:7], AF.Exp)
                    B.se.inc(_li)
                se_m = B.se.n

                @blk.vector
                def _(v, q=q, ab=ab, aw=aw, se_m=se_m):
                    v.wait_ge(B.se.h, se_m)
                    if q >= 2:
                        v.wait_ge(*ds_acc[q - 1])
                    aA = ncol[:, 5:6]; aB = ncol[:, 6:7]
                    v.tensor_mul(ncol[:, 7:8], aA, ncol[:, 3:4])
                    ssync(B, v, B.ve, v.scalar_tensor_tensor(ncol[:, 7:8], ncol[:, 1:2], aB,
                                                              ncol[:, 7:8], ALU.mult, ALU.add))
                    ssync(B, v, B.ve, v.reciprocal(ncol[:, 7:8], ncol[:, 7:8]))
                    v.tensor_scalar(aw[:, 0:512], ps_av[0][:], aB, None, ALU.mult)
                    v.tensor_scalar(aw[:, 512:1024], ps_av[1][:], aB, None, ALU.mult)
                    v.scalar_tensor_tensor(aw[:], ab[:], aA, aw[:], ALU.mult, ALU.add)
                    _li = v.tensor_scalar(aw[:], aw[:], ncol[:, 7:8], None, ALU.mult)
                    B.ve.inc(_li)
                ve_m2 = B.ve.n

                @blk.scalar
                def _(sc, so=so, ve_m2=ve_m2):
                    sc.wait_ge(B.ve.h, ve_m2)
                    _li = sc.copy(sowf[:], so[:])
                    B.se.inc(_li)
                se_c = B.se.n

                @blk.vector
                def _(v, aw=aw, se_c=se_c):
                    v.wait_ge(B.se.h, se_c)
                    v.tensor_add(aw[:], aw[:], sowf[:])
                    v.tensor_add(aw[:], aw[:], glob_bc[:])
                    _li = v.tensor_scalar(aw[:], aw[:], B.beta_c[:, 0:1], None, ALU.mult)
                    B.ve.inc(_li)
                ve_m3 = B.ve.n

                @blk.scalar
                def _(sc, q=q, aw=aw, ve_m3=ve_m3):
                    sc.wait_ge(B.ve.h, ve_m3)
                    _li = sc.activation(aw[:], aw[:], AF.Tanh)
                    B.se.inc(_li)
                se_t = B.se.n
                ve_acc_q.append(B.ve.n)

                @blk.sync
                def _(sync, q=q, aw=aw, se_t=se_t):
                    sync.wait_ge(B.se.h, se_t)
                    s = a_so.nxt()
                    s.inc(sync.dma_start(out=B.d("sattn")[q * 128:(q + 1) * 128, :], in_=aw[:]), 16)
                ds_acc.append(a_so.sems[q % 2].tok())



def _ph_moe0(B, blk):
    nc, I = B.nc, B.inp
    for lst in (pe_xn_done, ve_u_hist, ve_h_hist, ds_h_hist, pe_b2mm,
                ve_o_sub, dg_o_sub, ve_blkB, pe_q_hist, ld_q):
        lst.clear()
    with ExitStack() as ph:
        sb = lambda n, s, d=F32: ph.enter_context(nc.sbuf_tensor(n, s, d))
        pps = lambda n, s, d=F32: ph.enter_context(nc.psum_tensor(n, s, d))
        slotX = sb("m0_slotX", [128, IC * H0], BF16)
        slotY = sb("m0_slotY", [128, HC0 * D], BF16)
        xnb = [sb(f"m0_xn{i}", [128, IC * 512], BF16) for i in range(2)]
        hst = [sb(f"m0_h{i}", [128, 512], BF16) for i in range(2)]
        xf = [sb(f"m0_xf{i}", [128, 512]) for i in range(2)]
        sqg = [sb(f"m0_sq{i}", [128, 512]) for i in range(2)]
        sgg = [sb(f"m0_sg{i}", [128, 512]) for i in range(2)]
        hblk = [sb(f"m0_hb{i}", [128, 8 * 512], BF16) for i in range(2)]
        ob = [sb(f"m0_ob{i}", [128, D]) for i in range(2)]
        b2row = sb("m0_b2row", [128, D])
        b2stg = sb("m0_b2stg", [1, D])
        ps8 = [pps(f"m0_ps{i}", [128, 512], F32) for i in range(8)]
        ps_h = [ps8[0], ps8[1]]

        c_sx = DS(B, "c_sx", "sc")
        c_sy = DS(B, "c_sy", "sc")
        m_xn = DS(B, "m_xn", "sync", parity=2)
        m_ho = DS(B, "m_ho", "sync")
        m_hi = DS(B, "m_hi", "sync", parity=2)
        m_b2 = DS(B, "m_b2", "sync")
        m_go = DS(B, "m_go", "gp", parity=2)
        dc_slotX = []
        dc_slotY = []
        pe_useX = [0]
        pe_useY = [0]

        @blk.scalar
        def _(sc):
            sc.wait_ge(*B.tk_w1c)
            s = c_sx.nxt()
            for ic in range(IC):
                s.inc(sc.dma_start(out=slotX[:, ic * H0:(ic + 1) * H0],
                                   in_=B.d("w1bf")[0][:, ic * 128:(ic + 1) * 128], transpose=True), 16)
        dc_slotX.append(c_sx.sems[0].tok())

        @blk.scalar
        def _(sc):
            sc.wait_ge(*B.tk_w2c)
            s = c_sy.nxt()
            for hc in range(HC0):
                s.inc(sc.dma_start(out=slotY[:, hc * D:(hc + 1) * D],
                                   in_=B.d("w2bf")[0][:, hc * 128:(hc + 1) * 128], transpose=True), 16)
        dc_slotY.append(c_sy.sems[0].tok())

        ds_xn = []
        se_h = [0]
        ve_o = [0]
        dg_acc = [0]
        ds_hout = [0]
        ds_hin = []
        se_hist = []
        pe_hist = []

        for e in range(E):
            for b in range(NT_OWN):
                xb = xnb[b % 2]

                @blk.sync
                def _(sync, e=e, b=b, xb=xb):
                    k = e * NT_OWN + b
                    if k >= 2:
                        sync.wait_ge(B.pe.h, pe_xn_done[k - 2])
                    s = m_xn.nxt()
                    s.inc(sync.dma_start(out=xb[:].rearrange("p (o c) -> p o c", o=IC),
                                         in_=B.d("xn0").rearrange("o p c -> p o c")[:, :, b * 512:(b + 1) * 512]), 16)
                ds_x_t = m_xn.sems[(e * NT_OWN + b) % 2].tok()

                for hc in range(HC0):
                    pp = ps_h[hc % 2]
                    hs = hst[hc % 2]
                    xfb = xf[hc % 2]

                    @blk.tensor
                    def _(t, e=e, b=b, hc=hc, pp=pp, xb=xb, ds_x_t=ds_x_t):
                        if hc == 0:
                            t.wait_ge(*ds_x_t)
                            if b == 0:
                                t.wait_ge(*dc_slotX[e])
                                if e > 0:
                                    t.wait_ge(B.ve.h, ve_blkB[-1])
                        if len(se_hist) >= 2:
                            t.wait_ge(B.se.h, se_hist[-2])
                        ins = mm_group(t, pp[:],
                                       [(slotX[:, ic * H0 + hc * 128: ic * H0 + hc * 128 + 128],
                                         xb[:, ic * 512:(ic + 1) * 512]) for ic in range(IC)])
                        B.pe.inc(ins)
                    pe_h = B.pe.n

                    sqb = sqg[hc % 2]
                    sgb = sgg[hc % 2]

                    @blk.scalar
                    def _(sc, e=e, hc=hc, pp=pp, xfb=xfb, sqb=sqb, pe_h=pe_h):
                        sc.wait_ge(B.pe.h, pe_h)
                        if len(ve_h_hist) >= 2:
                            sc.wait_ge(B.ve.h, ve_h_hist[-2])
                        sc.add(xfb[:], pp[:], B.b1c[:, e * HC0 + hc: e * HC0 + hc + 1])
                        _li = sc.activation(sqb[:], xfb[:], AF.Square, scale=float(np.sqrt(C1)))
                        B.se.inc(_li)
                    se_hist.append(B.se.n)
                    se_1 = B.se.n

                    @blk.vector
                    def _(v, xfb=xfb, sqb=sqb, se_1=se_1):
                        v.wait_ge(B.se.h, se_1)
                        _li = v.scalar_tensor_tensor(sqb[:], sqb[:], 1.0, xfb[:], ALU.add, ALU.mult)
                        B.ve.inc(_li)
                    ve_u_hist.append(B.ve.n)
                    ve_u = B.ve.n

                    @blk.scalar
                    def _(sc, sqb=sqb, sgb=sgb, ve_u=ve_u):
                        sc.wait_ge(B.ve.h, ve_u)
                        _li = sc.activation(sgb[:], sqb[:], AF.Sigmoid, scale=S2)
                        B.se.inc(_li)
                    se_2 = B.se.n

                    @blk.vector
                    def _(v, e=e, b=b, hc=hc, hs=hs, xfb=xfb, sgb=sgb, se_2=se_2):
                        v.wait_ge(B.se.h, se_2)
                        k = len(ve_h_hist)
                        if k >= 2:
                            v.wait_ge(*ds_h_hist[k - 2])
                        _li = v.tensor_mul(hs[:], xfb[:], sgb[:])
                        B.ve.inc(_li)
                    ve_h_hist.append(B.ve.n)
                    ve_h = B.ve.n

                    @blk.sync
                    def _(sync, e=e, b=b, hc=hc, hs=hs, ve_h=ve_h):
                        sync.wait_ge(B.ve.h, ve_h)
                        s = m_ho.nxt()
                        s.inc(sync.dma_start(out=B.d("hsp")[hc][:, b * 512:(b + 1) * 512], in_=hs[:]), 16)
                    ds_h_hist.append(m_ho.sems[0].tok())
                pe_xn_done.append(B.pe.n)
            pe_useX.append(B.pe.n)
            ds_hA = m_ho.sems[0].tok()

            if e + 1 < E:
                @blk.scalar
                def _(sc, e=e):
                    sc.wait_ge(B.pe.h, pe_useX[e + 1])
                    s = c_sx.nxt()
                    for ic in range(IC):
                        s.inc(sc.dma_start(out=slotX[:, ic * H0:(ic + 1) * H0],
                                           in_=B.d("w1bf")[e + 1][:, ic * 128:(ic + 1) * 128],
                                           transpose=True), 16)
                dc_slotX.append(c_sx.sems[0].tok())

            @blk.sync
            def _(sync, e=e):
                sync.wait_ge(B.pe.h, pe_b2mm[-1] if pe_b2mm else 0)
                s = m_b2.nxt()
                s.inc(sync.dma_start(out=b2stg[:], in_=I["b2"][e:e + 1, :]), 16)
            ds_b2 = m_b2.sems[0].tok()

            @blk.tensor
            def _(t, e=e, ds_b2=ds_b2):
                t.wait_ge(*ds_b2)
                t.wait_ge(B.se.h, se_hist[-1])
                t.matmul(ps8[6][:], B.ones_f[:], b2stg[:, 0:512], start=True, stop=True)
                t.matmul(ps8[7][:], B.ones_f[:], b2stg[:, 512:1024], start=True, stop=True)
                B.pe.inc(t.drain())
            pe_b2mm.append(B.pe.n)
            pe_b2 = B.pe.n

            @blk.vector
            def _(v, pe_b2=pe_b2):
                v.wait_ge(B.pe.h, pe_b2)
                v.tensor_copy(b2row[:, 0:512], ps8[6][:])
                _li = v.tensor_copy(b2row[:, 512:1024], ps8[7][:])
                B.ve.inc(_li)
            ve_b2 = B.ve.n

            for b in range(NT_OWN):
                ds_q = []
                for kq in range(6):
                    if kq >= 2:
                        kqc = kq - 2
                        hb = hblk[kqc % 2]

                        @blk.tensor
                        def _(t, e=e, b=b, kqc=kqc, hb=hb, ds_q=tuple(ds_q), ve_b2=ve_b2):
                            t.wait_ge(*ds_q[kqc])
                            if kqc == 0:
                                t.wait_ge(B.ve.h, ve_b2)
                                t.wait_ge(B.ve.h, ve_blkB[-1] if ve_blkB else 0)
                                if b == 0:
                                    t.wait_ge(*dc_slotY[e])
                            for sub in range(4):
                                for hcl in range(8):
                                    hc = kqc * 8 + hcl
                                    for oh in range(2):
                                        _li = t.matmul(ps8[sub * 2 + oh][:],
                                                 hb[:, hcl * 512 + sub * 128: hcl * 512 + sub * 128 + 128],
                                                 slotY[:, hc * D + oh * 512: hc * D + oh * 512 + 512],
                                                 start=(hc == 0), stop=(hc == HC0 - 1))
                            B.pe.inc(_li)
                        pe_q_hist.append(B.pe.n)
                    if kq < 4:
                        hb = hblk[kq % 2]

                        @blk.sync
                        def _(sync, e=e, b=b, kq=kq, hb=hb, ds_hA=ds_hA):
                            if b == 0 and kq == 0:
                                sync.wait_ge(*ds_hA)
                            k = len(ld_q)
                            if k >= 2:
                                sync.wait_ge(B.pe.h, pe_q_hist[k - 2])
                            ld_q.append(k)
                            s = m_hi.nxt()
                            s.inc(sync.dma_start(
                                out=hb[:].rearrange("p (h c) -> p h c", h=8),
                                in_=B.d("hsp").rearrange("h p c -> p h c")
                                [:, kq * 8:(kq + 1) * 8, b * 512:(b + 1) * 512]), 16)
                        ds_q.append(m_hi.sems[kq % 2].tok())
                pe_B = B.pe.n

                for sub in range(4):
                    @blk.vector
                    def _(v, e=e, b=b, sub=sub, pe_B=pe_B):
                        v.wait_ge(B.pe.h, pe_B)
                        kk = len(ve_o_sub)
                        if kk >= 2:
                            v.wait_ge(*dg_o_sub[kk - 2])
                        o = ob[sub % 2]
                        qtile = b * 4 + sub
                        cwc = B.cw0[:, qtile * E + e: qtile * E + e + 1]
                        v.tensor_add(o[:, 0:512], ps8[sub * 2][:], b2row[:, 0:512])
                        v.tensor_add(o[:, 512:1024], ps8[sub * 2 + 1][:], b2row[:, 512:1024])
                        _li = v.tensor_scalar(o[:], o[:], cwc, None, ALU.mult)
                        B.ve.inc(_li)
                    ve_o_sub.append(B.ve.n)
                    ve_oo = B.ve.n

                    @blk.gpsimd
                    def _(gp, e=e, b=b, sub=sub, ve_oo=ve_oo):
                        gp.wait_ge(B.ve.h, ve_oo)
                        o = ob[sub % 2]
                        r0 = b * 512 + sub * 128
                        s = m_go.nxt()
                        if e == 0:
                            _di = gp.dma_start(out=B.d("g0")[r0:r0 + 128, :], in_=o[:])
                        else:
                            _di = gp.dma_start(out=B.d("g0")[r0:r0 + 128, :], in_=o[:],
                                               accum_op=ALU.add)
                        s.inc(_di, 16)
                    dg_o_sub.append(m_go.sems[len(dg_o_sub) % 2].tok())
                ve_blkB.append(B.ve.n)
            pe_useY.append(B.pe.n)

            if e + 1 < E:
                @blk.scalar
                def _(sc, e=e):
                    sc.wait_ge(B.pe.h, pe_useY[e + 1])
                    s = c_sy.nxt()
                    for hc in range(HC0):
                        s.inc(sc.dma_start(out=slotY[:, hc * D:(hc + 1) * D],
                                           in_=B.d("w2bf")[e + 1][:, hc * 128:(hc + 1) * 128],
                                           transpose=True), 16)
                dc_slotY.append(c_sy.sems[0].tok())


pe_xn_done = []
ve_u_hist = []
ve_h_hist = []
ds_h_hist = []
pe_b2mm = []
ve_o_sub = []
dg_o_sub = []
ve_blkB = []
pe_q_hist = []
ld_q = []



def _ph_combine0(B, blk):
    nc, I = B.nc, B.inp
    with ExitStack() as ph:
        sb = lambda n, s, d=F32: ph.enter_context(nc.sbuf_tensor(n, s, d))
        pps = lambda n, s, d=F32: ph.enter_context(nc.psum_tensor(n, s, d))
        xb = [sb(f"c0_x{i}", [128, 4 * D]) for i in range(2)]
        sbf = [sb(f"c0_s{i}", [128, 4 * D]) for i in range(2)]
        gb = [sb(f"c0_g{i}", [128, 4 * D]) for i in range(2)]
        ob = [sb(f"c0_o{i}", [128, 4 * D]) for i in range(2)]

        c_in = DS(B, "c0_in", "sync", parity=2)
        c_out = DS(B, "c0_out", "sync", parity=2)
        ds_in = []
        ve_done = [0, 0]
        ds_out2 = [None, None]
        for b in range(NT_OWN):
            p = b % 2

            @blk.sync
            def _(sync, b=b, p=p):
                if b >= 2:
                    sync.wait_ge(B.ve.h, ve_done[b])
                r0 = b * 512
                s = c_in.nxt()
                s.inc(sync.dma_start(out=xb[p][:].rearrange("p (s d) -> p s d", s=4),
                                     in_=I["xo"][r0:r0 + 512, :].rearrange("(s p) d -> p s d", p=128)), 16)
                s.inc(sync.dma_start(out=sbf[p][:].rearrange("p (s d) -> p s d", s=4),
                                     in_=B.d("sattn")[r0:r0 + 512, :].rearrange("(s p) d -> p s d", p=128)), 16)
                s.inc(sync.dma_start(out=gb[p][:].rearrange("p (s d) -> p s d", s=4),
                                     in_=B.d("g0")[r0:r0 + 512, :].rearrange("(s p) d -> p s d", p=128)), 16)
            ds_in.append(c_in.sems[b % 2].tok())

            @blk.scalar
            def _(sc, b=b, p=p):
                sc.wait_ge(*ds_in[b])
                _li = sc.activation(gb[p][:], gb[p][:], AF.Silu)
                B.se.inc(_li)
            se_g = B.se.n

            @blk.vector
            def _(v, b=b, p=p, se_g=se_g):
                v.wait_ge(B.se.h, se_g)
                if b >= 2:
                    v.wait_ge(*ds_out2[b])
                v.tensor_mul(ob[p][:], sbf[p][:], gb[p][:])
                _li = v.tensor_add(ob[p][:], ob[p][:], xb[p][:])
                B.ve.inc(_li)
            ve_done.append(B.ve.n)
            ve_c = B.ve.n

            @blk.sync
            def _(sync, b=b, p=p, ve_c=ve_c):
                sync.wait_ge(B.ve.h, ve_c)
                r0 = b * 512
                s = c_out.nxt()
                s.inc(sync.dma_start(out=B.d("x1own")[r0:r0 + 512, :].rearrange("(s p) d -> p s d", p=128),
                                     in_=ob[p][:].rearrange("p (s d) -> p s d", s=4)), 16)
            ds_out2.append(c_out.sems[b % 2].tok())



def _ph_startup_l1(B, blk):
    nc, I = B.nc, B.inp
    ph = ExitStack()
    pps = lambda n, s, d=F32: ph.enter_context(nc.psum_tensor(n, s, d))
    if B.mode == "l1":
        B.ident_bf = B.sbuf("ident_bf", [128, 128], BF16)
        B.ident_f = B.sbuf("ident_f", [128, 128], F32)
        B.ones_f = B.sbuf("ones_f", [1, 128], F32)
        B.eps_c = B.sbuf("eps_c", [128, 1], F32)

        s0 = DS(B, "l1_consts", "sync")

        @blk.sync
        def _(sync):
            s = s0.nxt()
            s.inc(sync.dma_start(out=B.ident_bf[:], in_=I["c_ident_bf"][:]), 16)
            s.inc(sync.dma_start(out=B.ident_f[:], in_=I["c_ident_f32"][:]), 16)
            s.inc(sync.dma_start(out=B.ones_f[:], in_=I["c_ones_f32"][:]), 16)
            s.inc(sync.dma_start(out=B.eps_c[:], in_=I["c_eps"][:]), 16)
    B.lng_bc = B.sbuf("lng_bc", [128, D], F32)
    B.lnb_bc = B.sbuf("lnb_bc", [128, D], F32)
    B.bqk_c = B.sbuf("bqk_c", [128, 1], F32)
    B.osw = B.sbuf("osw", [128, 2], F32)
    B.osb = B.sbuf("osb", [128, 2], F32)
    B.cw1 = B.sbuf("cw1", [128, QT * E], F32)
    B.r1p = B.sbuf("r1p", [128, IC * E], F32)
    B.sgwp = B.sbuf("sgwp", [128, IC], BF16)
    B.wqk = B.sbuf("wqk", [128, IC * 128], BF16)
    B.q1 = B.sbuf("q1", [128, HALF], BF16)
    B.k1 = B.sbuf("k1", [128, S], BF16)

    st2 = ph.enter_context(nc.sbuf_tensor("l1_st2", [2, 128], F32))
    st1 = ph.enter_context(nc.sbuf_tensor("l1_st1", [1, 128], F32))
    stg = ph.enter_context(nc.sbuf_tensor("l1_stg", [1, D], F32))
    ps_t = pps("l1su_ps", [128, 512], F32)
    ps_u = pps("l1su_ps2", [128, 512], F32)

    l1st = DS(B, "l1_stage", "sync")
    B.l1st = l1st

    @blk.sync
    def _(sync):
        s = l1st.nxt()
        for ic in range(IC):
            s.inc(sync.dma_start(out=B.r1p[:, ic * E:(ic + 1) * E],
                                 in_=I["r1w"][ic * 128:(ic + 1) * 128, :]), 16)
        s.inc(sync.dma_start(out=stg[:], in_=I["ln_scale"][:].rearrange("(a d) -> a d", a=1)), 16)
    ds_g = l1st.sems[0].tok()

    g_qk = DS(B, "g_qk", "gp")

    @blk.gpsimd
    def _(gp):
        s = g_qk.nxt()
        for ic in range(IC):
            s.inc(gp.dma_start(out=B.sgwp[:, ic:ic + 1], in_=I["sgw"][ic * 128:(ic + 1) * 128, :]), 16)
        for ic in range(IC):
            s.inc(gp.dma_start(out=B.wqk[:, ic * 128:(ic + 1) * 128],
                               in_=I["W_qk"][ic * 128:(ic + 1) * 128, :]), 16)
    B.tk_wqk = g_qk.sems[0].tok()

    @blk.tensor
    def _(t):
        t.wait_ge(*ds_g)
        t.matmul(ps_t[:], B.ones_f[:], stg[:, 0:512], start=True, stop=True)
        t.matmul(ps_u[:], B.ones_f[:], stg[:, 512:1024], start=True, stop=True)
        B.pe.inc(t.drain())
    pe_g = B.pe.n

    @blk.vector
    def _(v):
        v.wait_ge(B.pe.h, pe_g)
        v.tensor_copy(B.lng_bc[:, 0:512], ps_t[:])
        _li = v.tensor_copy(B.lng_bc[:, 512:1024], ps_u[:])
        B.ve.inc(_li)
    ve_g = B.ve.n

    @blk.sync
    def _(sync):
        sync.wait_ge(B.ve.h, ve_g)
        s = l1st.nxt()
        s.inc(sync.dma_start(out=stg[:], in_=I["ln_bias"][:].rearrange("(a d) -> a d", a=1)), 16)
    ds_b = l1st.sems[0].tok()

    @blk.tensor
    def _(t):
        t.wait_ge(*ds_b)
        t.matmul(ps_t[:], B.ones_f[:], stg[:, 0:512], start=True, stop=True)
        t.matmul(ps_u[:], B.ones_f[:], stg[:, 512:1024], start=True, stop=True)
        B.pe.inc(t.drain())
    pe_b = B.pe.n

    @blk.vector
    def _(v):
        v.wait_ge(B.pe.h, pe_b)
        v.tensor_copy(B.lnb_bc[:, 0:512], ps_t[:])
        _li = v.tensor_copy(B.lnb_bc[:, 512:1024], ps_u[:])
        B.ve.inc(_li)
    ve_b2 = B.ve.n

    @blk.sync
    def _(sync):
        sync.wait_ge(B.ve.h, ve_b2)
        s = l1st.nxt()
        s.inc(sync.dma_start(out=st1[:], in_=I["b_qk"][:].rearrange("(a d) -> a d", a=1)), 16)
        s.inc(sync.dma_start(out=st2[:], in_=I["os_weight"][:]), 16)
    ds_o = l1st.sems[0].tok()

    @blk.tensor
    def _(t):
        t.wait_ge(*ds_o)
        t.transpose(ps_t[:, 0:1], st1[:], B.ident_f[0:1, 0:1])
        t.transpose(ps_t[:, 8:10], st2[:], B.ident_f[0:2, 0:2])
        B.pe.inc(t.drain())
    pe_o = B.pe.n

    @blk.vector
    def _(v):
        v.wait_ge(B.pe.h, pe_o)
        v.tensor_copy(B.bqk_c[:], ps_t[:, 0:1])
        _li = v.tensor_copy(B.osw[:], ps_t[:, 8:10])
        B.ve.inc(_li)
    ve_o = B.ve.n

    @blk.sync
    def _(sync):
        sync.wait_ge(B.ve.h, ve_o)
        s = l1st.nxt()
        s.inc(sync.dma_start(out=st2[:], in_=I["os_bias"][:]), 16)
    ds_o2 = l1st.sems[0].tok()

    @blk.tensor
    def _(t):
        t.wait_ge(*ds_o2)
        t.transpose(ps_t[:, 8:10], st2[:], B.ident_f[0:2, 0:2])
        B.pe.inc(t.drain())
    pe_o2 = B.pe.n

    @blk.vector
    def _(v):
        v.wait_ge(B.pe.h, pe_o2)
        _li = v.tensor_copy(B.osb[:], ps_t[:, 8:10])
        B.ve.inc(_li)
    ph.close()



def _emit_ln12(B, blk, xin_t, xnf, x2f, sq, cols, ds_ready, pe_free):
    @blk.vector
    def _(v):
        if ds_ready is not None:
            v.wait_ge(*ds_ready)
        if pe_free:
            v.wait_ge(B.pe.h, pe_free)
        ssync(B, v, B.ve, v.reduce_sum(cols[:, 0:1], xin_t, axis=AX.X))
        ssync(B, v, B.ve, v.tensor_scalar_mul(cols[:, 1:2], cols[:, 0:1], -1.0 / D))
        v.tensor_scalar_add(xnf, xin_t, cols[:, 1:2])
        v.tensor_mul(sq, xnf, xnf)
        _li = v.reduce_sum(cols[:, 2:3], sq, axis=AX.X)
        B.ve.inc(_li)
    ve1 = B.ve.n

    @blk.scalar
    def _(sc):
        sc.wait_ge(B.ve.h, ve1)
        _li = sc.activation(cols[:, 3:4], cols[:, 2:3], AF.Sqrt, bias=B.eps_c[:, 0:1], scale=1.0 / D)
        B.se.inc(_li)
    se1 = B.se.n

    @blk.vector
    def _(v):
        v.wait_ge(B.se.h, se1)
        ssync(B, v, B.ve, v.reciprocal(cols[:, 3:4], cols[:, 3:4]))
        v.tensor_scalar_mul(xnf, xnf, cols[:, 3:4])
        ssync(B, v, B.ve, v.reduce_sum(cols[:, 4:5], xnf, axis=AX.X))
        ssync(B, v, B.ve, v.tensor_scalar_mul(cols[:, 5:6], cols[:, 4:5], -1.0 / D))
        v.tensor_scalar_add(x2f, xnf, cols[:, 5:6])
        v.tensor_mul(sq, x2f, x2f)
        _li = v.reduce_sum(cols[:, 6:7], sq, axis=AX.X)
        B.ve.inc(_li)
    ve2 = B.ve.n

    @blk.scalar
    def _(sc):
        sc.wait_ge(B.ve.h, ve2)
        _li = sc.activation(cols[:, 7:8], cols[:, 6:7], AF.Sqrt, bias=B.eps_c[:, 0:1], scale=1.0 / D)
        B.se.inc(_li)
    se2 = B.se.n

    @blk.vector
    def _(v):
        v.wait_ge(B.se.h, se2)
        ssync(B, v, B.ve, v.reciprocal(cols[:, 7:8], cols[:, 7:8]))
        v.scalar_tensor_tensor(x2f, x2f, cols[:, 7:8], B.lng_bc[:], ALU.mult, ALU.mult)
        _li = v.tensor_add(x2f, x2f, B.lnb_bc[:])
        B.ve.inc(_li)
    return B.ve.n


class Pipe:

    def __init__(self, B, banks):
        self.B = B
        self.banks = banks
        self.free = []

    def step(self, blk, pairs, evac, mm_waits=(), width=None):
        B = self.B
        i = len(self.free)
        bank = self.banks[i % len(self.banks)]
        nb = len(self.banks)
        guard = self.free[i - nb] if i >= nb else None

        @blk.tensor
        def _(t):
            for sem, val in mm_waits:
                t.wait_ge(sem, val)
            if guard is not None:
                t.wait_ge(B.ve.h if guard[0] == "ve" else B.se.h, guard[1])
            ins = mm_group(t, bank[:] if width is None else bank[:, 0:width], pairs)
            B.pe.inc(ins)
        pe_t = B.pe.n

        done = evac(blk, bank, pe_t)
        assert done[0] in ("ve", "se")
        self.free.append(done)
        return pe_t, done


def _ph_ln_own(B, blk, x1own_src):
    nc, I = B.nc, B.inp
    with ExitStack() as ph:
        sb = lambda n, s, d=F32: ph.enter_context(nc.sbuf_tensor(n, s, d))
        pps = lambda n, s, d=F32: ph.enter_context(nc.psum_tensor(n, s, d))
        whg = sb("lo_whg", [128, IC * HID], BF16)
        bhg_bc = sb("lo_bhg", [128, HID], F32)
        bh_row = sb("lo_bhrow", [1, HID], F32)
        xin = [sb(f"lo_x{i}", [128, 4 * D]) for i in range(2)]
        xnf = sb("lo_xnf", [128, D])
        x2f = sb("lo_x2f", [128, D])
        sq = sb("lo_sq", [128, D])
        cols = sb("lo_cols", [128, 16])
        xn1bf = sb("lo_xn1bf", [128, IC * 512], BF16)
        xn1f = sb("lo_xn1f", [128, IC * 512])
        xn2bf = sb("lo_xn2bf", [128, IC * 512], BF16)
        qkf = sb("lo_qkf", [128, 512])
        gst = [sb(f"lo_g{i}", [128, HID]) for i in range(2)]
        gbf = [sb(f"lo_gbf{i}", [128, HID], BF16) for i in range(2)]
        rsq = sb("lo_rsq", [128, 4 * E])
        ps_a = pps("lo_psa", [128, 512], F32)
        ps_b = pps("lo_psb", [128, 512], F32)
        ps_g = [pps(f"lo_psg{i}", [128, 512], F32) for i in range(2)]
        ps_r = pps("lo_psr", [128, 512], F32)

        g_whg = DS(B, "g_whg", "gp")

        @blk.gpsimd
        def _(gp):
            gp.wait_ge(*B.tk_wqk)
            s = g_whg.nxt()
            for ic in range(IC):
                s.inc(gp.dma_start(out=whg[:, ic * HID:(ic + 1) * HID],
                                   in_=I["W_hidden"][ic * 128:(ic + 1) * 128, HID:2 * HID]), 16)
        dg_whg = g_whg.sems[0].tok()

        lo_bh = DS(B, "lo_bh", "sync")

        @blk.sync
        def _(sync):
            s = lo_bh.nxt()
            s.inc(sync.dma_start(out=bh_row[:],
                                 in_=I["b_hidden"][:].rearrange("(a h) -> a h", a=1)[:, HID:2 * HID]), 16)
        ds_bh = lo_bh.sems[0].tok()
        bc = Pipe(B, ps_g)
        for c in range(4):
            def ev(blk2, bank, pe_t, c=c):
                @blk2.vector
                def _(v):
                    v.wait_ge(B.pe.h, pe_t)
                    _li = v.tensor_copy(bhg_bc[:, c * 512:(c + 1) * 512], bank[:])
                    B.ve.inc(_li)
                return ("ve", B.ve.n)
            bc.step(blk, [(B.ones_f[:], bh_row[:, c * 512:(c + 1) * 512])], ev,
                    mm_waits=[ds_bh])
        gate_pipe = Pipe(B, ps_g)
        gate_pipe.free.extend(bc.free[-2:])

        lo_x = DS(B, "lo_x", "sync", parity=2)
        lo_o = DS(B, "lo_o", "sync")
        lo_g = DS(B, "lo_g", "sync", parity=2)
        trh_pe, trh_ve = [], []
        tr2_pe, tr2_ve = [], []
        pe_mm_end = [0]
        ds_out = [None]
        ds_x = []
        se_qk = [0]
        se_sil = [0, 0]
        ds_gate = [None, None]

        rpipe = Pipe(B, [ps_r])

        for t_ in range(NT_OWN):
            p = t_ % 2

            @blk.sync
            def _(sync, t_=t_, p=p):
                if t_ >= 2:
                    sync.wait_ge(B.pe.h, pe_mm_end[t_ - 1])
                r0 = t_ * 512
                s = lo_x.nxt()
                s.inc(sync.dma_start(out=xin[p][:].rearrange("p (s d) -> p s d", s=4),
                                     in_=x1own_src[r0:r0 + 512, :].rearrange("(s p) d -> p s d", p=128)), 16)
            ds_x.append(lo_x.sems[t_ % 2].tok())

            for sub in range(4):
                xv = xin[p][:, sub * D:(sub + 1) * D]
                pe_free = pe_mm_end[t_] if (sub == 0 and t_ > 0) else 0
                ve_x2 = _emit_ln12(B, blk, xv, xnf[:], x2f[:], sq[:], cols[:],
                                   ds_x[t_] if sub == 0 else None, pe_free)

                for ic in range(IC):
                    pp = ps_a if ic % 2 == 0 else ps_b

                    @blk.tensor
                    def _(tt, ic=ic, pp=pp, ve_x2=ve_x2, sub=sub):
                        tt.wait_ge(B.ve.h, ve_x2)
                        if sub == 0 and ic == 0:
                            tt.wait_ge(B.se.h, se_qk[-1])
                        if len(trh_pe) >= 2:
                            tt.wait_ge(B.ve.h, trh_ve[-2])
                        _li = tt.transpose(pp[:, 0:128], xnf[:, ic * 128:(ic + 1) * 128],
                                     B.ident_f[:])
                        B.pe.inc(_li)
                    trh_pe.append(B.pe.n)

                    @blk.vector
                    def _(v, ic=ic, sub=sub, pp=pp, t_=t_):
                        v.wait_ge(B.pe.h, trh_pe[-1])
                        if t_ > 0 and len(trh_ve) % 32 == 0:
                            v.wait_ge(B.pe.h, pe_mm_end[t_])
                            v.wait_ge(*ds_out[t_])
                        v.tensor_copy(xn1bf[:, ic * 512 + sub * 128: ic * 512 + sub * 128 + 128],
                                      pp[:, 0:128])
                        _li = v.tensor_copy(xn1f[:, ic * 512 + sub * 128: ic * 512 + sub * 128 + 128],
                                      pp[:, 0:128])
                        B.ve.inc(_li)
                    trh_ve.append(B.ve.n)

                for ic in range(IC):
                    pp = ps_a if ic % 2 == 0 else ps_b

                    @blk.tensor
                    def _(tt, ic=ic, pp=pp):
                        tt.wait_ge(B.ve.h, trh_ve[-1] if ic == 0 else B.ve.n)
                        if len(tr2_pe) >= 2:
                            tt.wait_ge(B.ve.h, tr2_ve[-2])
                        _li = tt.transpose(pp[:, 0:128], x2f[:, ic * 128:(ic + 1) * 128],
                                     B.ident_f[:])
                        B.pe.inc(_li)
                    tr2_pe.append(B.pe.n)

                    @blk.vector
                    def _(v, ic=ic, sub=sub, pp=pp):
                        v.wait_ge(B.pe.h, tr2_pe[-1])
                        _li = v.tensor_copy(xn2bf[:, ic * 512 + sub * 128: ic * 512 + sub * 128 + 128],
                                      pp[:, 0:128])
                        B.ve.inc(_li)
                    tr2_ve.append(B.ve.n)
            ve_xn_all = B.ve.n

            @blk.tensor
            def _(tt, t_=t_, ve_xn_all=ve_xn_all):
                tt.wait_ge(B.ve.h, ve_xn_all)
                tt.wait_ge(*B.tk_wqk)
                tt.wait_ge(B.se.h, se_qk[-1])
                ins = mm_group(tt, ps_a[:],
                               [(B.wqk[:, ic * 128:(ic + 1) * 128],
                                 xn2bf[:, ic * 512:(ic + 1) * 512]) for ic in range(IC)])
                B.pe.inc(ins)
            pe_qk = B.pe.n

            @blk.scalar
            def _(sc, pe_qk=pe_qk):
                sc.wait_ge(B.pe.h, pe_qk)
                _li = sc.activation(qkf[:], ps_a[:], AF.Silu, bias=B.bqk_c[:, 0:1])
                B.se.inc(_li)
            se_qk.append(B.se.n)
            se_q = B.se.n

            @blk.vector
            def _(v, t_=t_, se_q=se_q):
                v.wait_ge(B.se.h, se_q)
                _li = v.tensor_scalar(B.q1[:, t_ * 512:(t_ + 1) * 512], qkf[:],
                                B.osw[:, 0:1], B.osb[:, 0:1], ALU.mult, ALU.add)
                B.ve.inc(_li)

            for sub in range(4):
                gi = t_ * 4 + sub
                gs = gst[gi % 2]
                gb = gbf[gi % 2]
                for hc in range(4):
                    def ev(blk2, bank, pe_t, gs=gs, hc=hc, gi=gi):
                        @blk2.vector
                        def _(v):
                            v.wait_ge(B.pe.h, pe_t)
                            if hc == 0:
                                v.wait_ge(B.se.h, se_sil[gi])
                            _li = v.tensor_add(gs[:, hc * 512:(hc + 1) * 512], bank[:],
                                         bhg_bc[:, hc * 512:(hc + 1) * 512])
                            B.ve.inc(_li)
                        return ("ve", B.ve.n)
                    gate_pipe.step(
                        blk,
                        [(xn2bf[:, ic * 512 + sub * 128: ic * 512 + sub * 128 + 128],
                          whg[:, ic * HID + hc * 512: ic * HID + hc * 512 + 512])
                         for ic in range(IC)],
                        ev,
                        mm_waits=[dg_whg])
                ve_g4 = B.ve.n

                @blk.scalar
                def _(sc, gi=gi, gs=gs, gb=gb, ve_g4=ve_g4):
                    sc.wait_ge(B.ve.h, ve_g4)
                    if gi >= 2:
                        sc.wait_ge(*ds_gate[gi])
                    _li = sc.activation(gb[:], gs[:], AF.Silu)
                    B.se.inc(_li)
                se_sil.append(B.se.n)
                se_s = B.se.n

                @blk.sync
                def _(sync, gi=gi, gb=gb, se_s=se_s):
                    sync.wait_ge(B.se.h, se_s)
                    r0 = gi * 128
                    s = lo_g.nxt()
                    s.inc(sync.dma_start(out=B.d("gate1")[r0:r0 + 128, :], in_=gb[:]), 16)
                ds_gate.append(lo_g.sems[gi % 2].tok())

            for sub in range(4):
                def ev(blk2, bank, pe_t, t_=t_, sub=sub):
                    @blk2.vector
                    def _(v):
                        v.wait_ge(B.pe.h, pe_t)
                        lg = rsq[:, 0:E]
                        v.tensor_copy(lg, bank[:, 0:E])
                        _li = v.reduce_max(cols[:, 8:9], lg, axis=AX.X, negate=True)
                        B.ve.inc(_li)
                    ve_a = B.ve.n

                    @blk2.scalar
                    def _(sc):
                        sc.wait_ge(B.ve.h, ve_a)
                        _li = sc.activation(rsq[:, E:2 * E], rsq[:, 0:E], AF.Exp,
                                      bias=cols[:, 8:9], accum_out=cols[:, 9:10])
                        B.se.inc(_li)
                    se_a = B.se.n

                    @blk2.vector
                    def _(v, t_=t_, sub=sub):
                        v.wait_ge(B.se.h, se_a)
                        pr = rsq[:, 2 * E:3 * E]
                        m1 = cols[:, 10:11]; m2 = cols[:, 11:12]
                        ssync(B, v, B.ve, v.reciprocal(cols[:, 9:10], cols[:, 9:10]))
                        v.tensor_scalar(pr, rsq[:, E:2 * E], cols[:, 9:10], None, ALU.mult)
                        ssync(B, v, B.ve, v.reduce_max(m1, pr, axis=AX.X))
                        v.tensor_scalar(rsq[:, 0:E], pr, m1, None, ALU.is_ge)
                        v.scalar_tensor_tensor(rsq[:, E:2 * E], rsq[:, 0:E], -1e30, pr,
                                               ALU.mult, ALU.add)
                        ssync(B, v, B.ve, v.reduce_max(m2, rsq[:, E:2 * E], axis=AX.X))
                        v.tensor_scalar(rsq[:, 3 * E:4 * E], rsq[:, E:2 * E], m2, None,
                                        ALU.is_ge)
                        qtile = t_ * 4 + sub
                        cw = B.cw1[:, qtile * E:(qtile + 1) * E]
                        v.tensor_scalar(rsq[:, 3 * E:4 * E], rsq[:, 3 * E:4 * E], m2,
                                        None, ALU.mult)
                        _li = v.scalar_tensor_tensor(cw, rsq[:, 0:E], m1, rsq[:, 3 * E:4 * E],
                                               ALU.mult, ALU.add)
                        B.ve.inc(_li)
                    return ("ve", B.ve.n)
                rpipe.step(
                    blk,
                    [(xn1f[:, ic * 512 + sub * 128: ic * 512 + sub * 128 + 128],
                      B.r1p[:, ic * E:(ic + 1) * E]) for ic in range(IC)],
                    ev,
                    mm_waits=[(B.ve.h, trh_ve[-1])], width=E)
            pe_mm_end.append(B.pe.n)
            ve_tile = B.ve.n

            @blk.sync
            def _(sync, t_=t_, ve_tile=ve_tile):
                sync.wait_ge(B.ve.h, ve_tile)
                s = lo_o.nxt()
                s.inc(sync.dma_start(
                    out=B.d("xn1").rearrange("o p c -> p o c")[:, :, t_ * 512:(t_ + 1) * 512],
                    in_=xn1bf[:].rearrange("p (o c) -> p o c", o=IC)), 16)
                s.inc(sync.dma_start(
                    out=B.d("xn2o").rearrange("o p c -> p o c")[:, :, t_ * 512:(t_ + 1) * 512],
                    in_=xn2bf[:].rearrange("p (o c) -> p o c", o=IC)), 16)
            ds_out.append(lo_o.sems[0].tok())

        @blk.sync
        def _(sync):
            sync.wait_ge(B.ve.h, B.ve.n)
            s = lo_o.nxt()
            s.inc(sync.dma_start(out=B.d("cw1d")[:].rearrange("(q p) e -> p q e", p=128),
                                 in_=B.cw1[:].rearrange("p (q e) -> p q e", e=E)), 16)



def _ph_moe1(B, blk):
    nc, I = B.nc, B.inp
    with ExitStack() as ph:
        sb = lambda n, s, d=F32: ph.enter_context(nc.sbuf_tensor(n, s, d))
        pps = lambda n, s, d=F32: ph.enter_context(nc.psum_tensor(n, s, d))
        w1p = [sb(f"m1_w1{i}", [128, IC * F1], BF16) for i in range(2)]
        w2p = [sb(f"m1_w2{i}", [128, IC * D], BF16) for i in range(2)]
        xnb = [sb(f"m1_xn{i}", [128, IC * 512], BF16) for i in range(2)]
        hfm = [sb(f"m1_h{i}", [128, IC * 512], BF16) for i in range(2)]
        ob = [sb(f"m1_ob{i}", [128, D]) for i in range(2)]
        ps_h = [pps(f"m1_ph{i}", [128, 512], F32) for i in range(4)]
        ps_o = [pps(f"m1_po{i}", [128, 512], F32) for i in range(4)]

        g_mw = DS(B, "g_mw", "gp", parity=2)
        m1_xn = DS(B, "m1_xn", "sync", parity=2)
        m1_go = DS(B, "m1_go", "gp", parity=2)
        dg_w = []
        pe_use = [0]
        se_h = []
        ve_o = []
        dg_o = []
        pe_blk = [0]

        @blk.gpsimd
        def _(gp):
            gp.wait_ge(*B.tk_wqk)
            s = g_mw.nxt()
            for ic in range(IC):
                s.inc(gp.dma_start(out=w1p[0][:, ic * F1:(ic + 1) * F1],
                                   in_=I["l1w1"][0][ic * 128:(ic + 1) * 128, :]), 16)
            for fc in range(IC):
                s.inc(gp.dma_start(out=w2p[0][:, fc * D:(fc + 1) * D],
                                   in_=I["l1w2"][0][fc * 128:(fc + 1) * 128, :]), 16)
        dg_w.append(g_mw.sems[0].tok())

        for e in range(E):
            sx = w1p[e % 2]
            sy = w2p[e % 2]
            for b in range(NT_OWN):
                k = e * NT_OWN + b
                xb = xnb[k % 2]

                @blk.sync
                def _(sync, e=e, b=b, k=k, xb=xb):
                    if k >= 2:
                        sync.wait_ge(B.pe.h, pe_blk[k - 1])
                    s = m1_xn.nxt()
                    s.inc(sync.dma_start(out=xb[:].rearrange("p (o c) -> p o c", o=IC),
                                         in_=B.d("xn1").rearrange("o p c -> p o c")[:, :, b * 512:(b + 1) * 512]), 16)
                ds_x = m1_xn.sems[k % 2].tok()
                hb = hfm[k % 2]

                for fc in range(IC):
                    pp = ps_h[fc % 4]

                    @blk.tensor
                    def _(t, e=e, fc=fc, pp=pp, xb=xb, ds_x=ds_x, sx=sx, k=k):
                        if fc == 0:
                            t.wait_ge(*ds_x)
                            if k % NT_OWN == 0:
                                t.wait_ge(*dg_w[e])
                        kk = len(se_h)
                        if kk >= 4:
                            t.wait_ge(B.se.h, se_h[kk - 4])
                        ins = mm_group(t, pp[:],
                                       [(sx[:, ic * F1 + fc * 128: ic * F1 + fc * 128 + 128],
                                         xb[:, ic * 512:(ic + 1) * 512]) for ic in range(IC)])
                        B.pe.inc(ins)
                    pe_1 = B.pe.n

                    @blk.scalar
                    def _(sc, fc=fc, pp=pp, hb=hb, pe_1=pe_1, k=k):
                        sc.wait_ge(B.pe.h, pe_1)
                        if fc == 0 and k >= 2:
                            sc.wait_ge(B.pe.h, pe_blk[k - 1])
                        _li = sc.activation(hb[:, fc * 512:(fc + 1) * 512], pp[:], AF.Silu)
                        B.se.inc(_li)
                    se_h.append(B.se.n)
                se_hb = B.se.n

                for sub in range(4):
                    pp = [ps_o[(sub % 2) * 2], ps_o[(sub % 2) * 2 + 1]]

                    @blk.tensor
                    def _(t, e=e, b=b, sub=sub, pp=pp, hb=hb, sy=sy, se_hb=se_hb):
                        if sub == 0:
                            t.wait_ge(B.se.h, se_hb)
                        kk = len(ve_o)
                        if kk >= 2:
                            t.wait_ge(B.ve.h, ve_o[kk - 2])
                        for fc in range(IC):
                            for oh in range(2):
                                _li = t.matmul(pp[oh][:],
                                         hb[:, fc * 512 + sub * 128: fc * 512 + sub * 128 + 128],
                                         sy[:, fc * D + oh * 512: fc * D + oh * 512 + 512],
                                         start=(fc == 0), stop=(fc == IC - 1))
                        B.pe.inc(_li)
                    pe_2 = B.pe.n

                    @blk.vector
                    def _(v, e=e, b=b, sub=sub, pp=pp, pe_2=pe_2):
                        v.wait_ge(B.pe.h, pe_2)
                        kk = len(ve_o)
                        if kk >= 2:
                            v.wait_ge(*dg_o[kk - 2])
                        o = ob[(b * 4 + sub) % 2]
                        qtile = b * 4 + sub
                        cwc = B.cw1[:, qtile * E + e: qtile * E + e + 1]
                        v.tensor_scalar(o[:, 0:512], pp[0][:], cwc, None, ALU.mult)
                        _li = v.tensor_scalar(o[:, 512:1024], pp[1][:], cwc, None, ALU.mult)
                        B.ve.inc(_li)
                    ve_o.append(B.ve.n)
                    ve_oo = B.ve.n

                    @blk.gpsimd
                    def _(gp, e=e, b=b, sub=sub, ve_oo=ve_oo):
                        gp.wait_ge(B.ve.h, ve_oo)
                        o = ob[(b * 4 + sub) % 2]
                        r0 = b * 512 + sub * 128
                        s = m1_go.nxt()
                        if e == 0:
                            _di = gp.dma_start(out=B.d("moe1")[r0:r0 + 128, :], in_=o[:])
                        else:
                            _di = gp.dma_start(out=B.d("moe1")[r0:r0 + 128, :], in_=o[:],
                                               accum_op=ALU.add)
                        s.inc(_di, 16)
                    dg_o.append(m1_go.sems[len(dg_o) % 2].tok())
                pe_blk.append(B.pe.n)
            pe_use.append(B.pe.n)

            if e + 1 < E:
                @blk.gpsimd
                def _(gp, e=e):
                    gp.wait_ge(B.pe.h, pe_use[e])
                    nx = w1p[(e + 1) % 2]
                    ny = w2p[(e + 1) % 2]
                    s = g_mw.nxt()
                    for ic in range(IC):
                        s.inc(gp.dma_start(out=nx[:, ic * F1:(ic + 1) * F1],
                                           in_=I["l1w1"][e + 1][ic * 128:(ic + 1) * 128, :]), 16)
                    for fc in range(IC):
                        s.inc(gp.dma_start(out=ny[:, fc * D:(fc + 1) * D],
                                           in_=I["l1w2"][e + 1][fc * 128:(fc + 1) * 128, :]), 16)
                dg_w.append(g_mw.sems[(e + 1) % 2].tok())

        with ExitStack() as ph2:
            sgp = ph2.enter_context(nc.sbuf_tensor("m1_sgp", [128, IC * 2 * F1], BF16))
            sdp = ph2.enter_context(nc.sbuf_tensor("m1_sdp", [128, IC * D], BF16))
            sgt = [ph2.enter_context(nc.sbuf_tensor(f"m1_sgt{i}", [128, 512], F32))
                   for i in range(2)]
            shst = ph2.enter_context(nc.sbuf_tensor("m1_sh", [128, IC * 512], BF16))
            sgc = ph2.enter_context(nc.sbuf_tensor("m1_sgc", [128, 4], F32))

            g_sh = DS(B, "g_sh", "gp")

            @blk.gpsimd
            def _(gp):
                gp.wait_ge(B.pe.h, pe_use[E - 1])
                s = g_sh.nxt()
                for ic in range(IC):
                    s.inc(gp.dma_start(out=sgp[:, ic * 2 * F1:(ic + 1) * 2 * F1],
                                       in_=I["sgu"][ic * 128:(ic + 1) * 128, :]), 16)
                for fc in range(IC):
                    s.inc(gp.dma_start(out=sdp[:, fc * D:(fc + 1) * D],
                                       in_=I["sdown"][fc * 128:(fc + 1) * 128, :]), 16)
            dg_s = g_sh.sems[0].tok()

            se_sg = []
            ve_sh = []
            se_gc = [0]
            for b in range(NT_OWN):
                k = E * NT_OWN + b
                xb = xnb[k % 2]

                @blk.sync
                def _(sync, b=b, k=k, xb=xb):
                    sync.wait_ge(B.pe.h, pe_blk[k - 1])
                    s = m1_xn.nxt()
                    s.inc(sync.dma_start(out=xb[:].rearrange("p (o c) -> p o c", o=IC),
                                         in_=B.d("xn1").rearrange("o p c -> p o c")[:, :, b * 512:(b + 1) * 512]), 16)
                ds_x = m1_xn.sems[k % 2].tok()

                for fc in range(IC):
                    pg = ps_h[(2 * fc) % 4]
                    pu = ps_h[(2 * fc + 1) % 4]

                    @blk.tensor
                    def _(t, b=b, fc=fc, pg=pg, pu=pu, xb=xb, ds_x=ds_x):
                        if fc == 0:
                            t.wait_ge(*ds_x)
                            t.wait_ge(*dg_s)
                        kk = len(ve_sh)
                        if kk >= 2:
                            t.wait_ge(B.ve.h, ve_sh[kk - 2])
                        ins = mm_group(t, pg[:],
                                       [(sgp[:, ic * 2 * F1 + fc * 128: ic * 2 * F1 + fc * 128 + 128],
                                         xb[:, ic * 512:(ic + 1) * 512]) for ic in range(IC)])
                        ins = mm_group(t, pu[:],
                                       [(sgp[:, ic * 2 * F1 + F1 + fc * 128: ic * 2 * F1 + F1 + fc * 128 + 128],
                                         xb[:, ic * 512:(ic + 1) * 512]) for ic in range(IC)])
                        B.pe.inc(ins)
                    pe_gu = B.pe.n

                    @blk.scalar
                    def _(sc, fc=fc, pg=pg, pe_gu=pe_gu, b=b):
                        sc.wait_ge(B.pe.h, pe_gu)
                        kk = len(se_sg)
                        if kk >= 2:
                            sc.wait_ge(B.ve.h, ve_sh[kk - 2] if kk >= 2 else 0)
                        _li = sc.activation(sgt[fc % 2][:], pg[:], AF.Silu)
                        B.se.inc(_li)
                    se_sg.append(B.se.n)
                    se_g1 = B.se.n

                    @blk.vector
                    def _(v, b=b, fc=fc, pu=pu, se_g1=se_g1):
                        v.wait_ge(B.se.h, se_g1)
                        if fc == 0 and b > 0:
                            v.wait_ge(B.pe.h, pe_blk[-1])
                        _li = v.tensor_mul(shst[:, fc * 512:(fc + 1) * 512], sgt[fc % 2][:], pu[:])
                        B.ve.inc(_li)
                    ve_sh.append(B.ve.n)
                ve_shb = B.ve.n

                @blk.tensor
                def _(t, b=b, xb=xb, ve_shb=ve_shb):
                    t.wait_ge(B.ve.h, ve_shb)
                    t.wait_ge(B.se.h, se_gc[-1])
                    for sub in range(4):
                        ins = mm_group(t, ps_h[0][:, sub:sub + 1],
                                       [(xb[:, ic * 512 + sub * 128: ic * 512 + sub * 128 + 128],
                                         B.sgwp[:, ic:ic + 1]) for ic in range(IC)])
                    B.pe.inc(ins)
                pe_sg = B.pe.n

                @blk.scalar
                def _(sc, pe_sg=pe_sg):
                    sc.wait_ge(B.pe.h, pe_sg)
                    _li = sc.activation(sgc[:], ps_h[0][:, 0:4], AF.Sigmoid)
                    B.se.inc(_li)
                se_gc.append(B.se.n)
                se_c = B.se.n

                for sub in range(4):
                    pp = [ps_o[(sub % 2) * 2], ps_o[(sub % 2) * 2 + 1]]

                    @blk.tensor
                    def _(t, b=b, sub=sub, pp=pp, se_c=se_c):
                        if sub == 0:
                            t.wait_ge(B.se.h, se_c)
                        kk = len(ve_o)
                        if kk >= 2:
                            t.wait_ge(B.ve.h, ve_o[kk - 2])
                        for fc in range(IC):
                            for oh in range(2):
                                _li = t.matmul(pp[oh][:],
                                         shst[:, fc * 512 + sub * 128: fc * 512 + sub * 128 + 128],
                                         sdp[:, fc * D + oh * 512: fc * D + oh * 512 + 512],
                                         start=(fc == 0), stop=(fc == IC - 1))
                        B.pe.inc(_li)
                    pe_d = B.pe.n

                    @blk.vector
                    def _(v, b=b, sub=sub, pp=pp, pe_d=pe_d):
                        v.wait_ge(B.pe.h, pe_d)
                        kk = len(ve_o)
                        if kk >= 2:
                            v.wait_ge(*dg_o[kk - 2])
                        o = ob[(b * 4 + sub) % 2]
                        v.tensor_scalar(o[:, 0:512], pp[0][:], sgc[:, sub:sub + 1], None, ALU.mult)
                        _li = v.tensor_scalar(o[:, 512:1024], pp[1][:], sgc[:, sub:sub + 1], None, ALU.mult)
                        B.ve.inc(_li)
                    ve_o.append(B.ve.n)
                    ve_oo = B.ve.n

                    @blk.gpsimd
                    def _(gp, b=b, sub=sub, ve_oo=ve_oo):
                        gp.wait_ge(B.ve.h, ve_oo)
                        o = ob[(b * 4 + sub) % 2]
                        r0 = b * 512 + sub * 128
                        s = m1_go.nxt()
                        s.inc(gp.dma_start(out=B.d("moe1")[r0:r0 + 128, :], in_=o[:],
                                           accum_op=ALU.add), 16)
                    dg_o.append(m1_go.sems[len(dg_o) % 2].tok())
                pe_blk.append(B.pe.n)



def _ph_ln_full(B, blk, x1full_src, wait_cc):
    nc, I = B.nc, B.inp
    with ExitStack() as ph:
        sb = lambda n, s, d=F32: ph.enter_context(nc.sbuf_tensor(n, s, d))
        pps = lambda n, s, d=F32: ph.enter_context(nc.psum_tensor(n, s, d))
        whv = sb("lf_whv", [128, IC * HID], BF16)
        bhv_bc = sb("lf_bhv", [128, HID], F32)
        bh_row = sb("lf_bhrow", [1, HID], F32)
        xin = [sb(f"lf_x{i}", [128, 4 * D]) for i in range(2)]
        xnf = sb("lf_xnf", [128, D])
        x2f = sb("lf_x2f", [128, D])
        sq = sb("lf_sq", [128, D])
        cols = sb("lf_cols", [128, 16])
        xn2bf = sb("lf_xn2bf", [128, IC * 512], BF16)
        qkf = sb("lf_qkf", [128, 512])
        vtmp = [sb(f"lf_vt{i}", [128, HID]) for i in range(2)]
        vst = [sb(f"lf_vs{i}", [128, HID], BF16) for i in range(2)]
        ps_a = pps("lf_psa", [128, 512], F32)
        ps_b = pps("lf_psb", [128, 512], F32)
        ps_g = [pps(f"lf_psg{i}", [128, 512], F32) for i in range(2)]

        g_whv = DS(B, "g_whv", "gp")

        @blk.gpsimd
        def _(gp):
            gp.wait_ge(*B.tk_wqk)
            s = g_whv.nxt()
            for ic in range(IC):
                s.inc(gp.dma_start(out=whv[:, ic * HID:(ic + 1) * HID],
                                   in_=I["W_hidden"][ic * 128:(ic + 1) * 128, 0:HID]), 16)
        dg_whv = g_whv.sems[0].tok()

        lf_bh = DS(B, "lf_bh", "sync")

        @blk.sync
        def _(sync):
            s = lf_bh.nxt()
            s.inc(sync.dma_start(out=bh_row[:],
                                 in_=I["b_hidden"][:].rearrange("(a h) -> a h", a=1)[:, 0:HID]), 16)
        ds_bh = lf_bh.sems[0].tok()
        bc = Pipe(B, ps_g)
        for c in range(4):
            def ev(blk2, bank, pe_t, c=c):
                @blk2.vector
                def _(v):
                    v.wait_ge(B.pe.h, pe_t)
                    _li = v.tensor_copy(bhv_bc[:, c * 512:(c + 1) * 512], bank[:])
                    B.ve.inc(_li)
                return ("ve", B.ve.n)
            bc.step(blk, [(B.ones_f[:], bh_row[:, c * 512:(c + 1) * 512])], ev,
                    mm_waits=[ds_bh])
        vpipe = Pipe(B, ps_g)
        vpipe.free.extend(bc.free[-2:])

        lf_x = DS(B, "lf_x", "sync", parity=2)
        lf_v = DS(B, "lf_v", "sync", parity=2)
        tr2_pe, tr2_ve = [], []
        pe_mm_end = [0]
        ds_x = []
        se_qk = [0]
        se_v = [0, 0]
        ds_v = [None, None]

        for t_ in range(NT_FULL):
            p = t_ % 2

            @blk.sync
            def _(sync, t_=t_, p=p):
                if t_ == 0 and wait_cc:
                    sync.wait_ge(B.ccs.h, 1)
                if t_ >= 2:
                    sync.wait_ge(B.pe.h, pe_mm_end[t_ - 1])
                r0 = t_ * 512
                s = lf_x.nxt()
                s.inc(sync.dma_start(out=xin[p][:].rearrange("p (s d) -> p s d", s=4),
                                     in_=x1full_src[r0:r0 + 512, :].rearrange("(s p) d -> p s d", p=128)), 16)
            ds_x.append(lf_x.sems[t_ % 2].tok())

            for sub in range(4):
                xv = xin[p][:, sub * D:(sub + 1) * D]
                pe_free = pe_mm_end[t_] if (sub == 0 and t_ > 0) else 0
                ve_x2 = _emit_ln12(B, blk, xv, xnf[:], x2f[:], sq[:], cols[:],
                                   ds_x[t_] if sub == 0 else None, pe_free)

                for ic in range(IC):
                    pp = ps_a if ic % 2 == 0 else ps_b

                    @blk.tensor
                    def _(tt, ic=ic, pp=pp, ve_x2=ve_x2, sub=sub):
                        tt.wait_ge(B.ve.h, ve_x2)
                        if sub == 0 and ic == 0:
                            tt.wait_ge(B.se.h, se_qk[-1])
                        if len(tr2_pe) >= 2:
                            tt.wait_ge(B.ve.h, tr2_ve[-2])
                        _li = tt.transpose(pp[:, 0:128], x2f[:, ic * 128:(ic + 1) * 128],
                                     B.ident_f[:])
                        B.pe.inc(_li)
                    tr2_pe.append(B.pe.n)

                    @blk.vector
                    def _(v, ic=ic, sub=sub, pp=pp, t_=t_):
                        v.wait_ge(B.pe.h, tr2_pe[-1])
                        if t_ > 0 and len(tr2_ve) % 32 == 0:
                            v.wait_ge(B.pe.h, pe_mm_end[t_])
                        _li = v.tensor_copy(xn2bf[:, ic * 512 + sub * 128: ic * 512 + sub * 128 + 128],
                                      pp[:, 0:128])
                        B.ve.inc(_li)
                    tr2_ve.append(B.ve.n)
            ve_xn_all = B.ve.n

            @blk.tensor
            def _(tt, t_=t_, ve_xn_all=ve_xn_all):
                tt.wait_ge(B.ve.h, ve_xn_all)
                tt.wait_ge(*B.tk_wqk)
                tt.wait_ge(B.se.h, se_qk[-1])
                ins = mm_group(tt, ps_a[:],
                               [(B.wqk[:, ic * 128:(ic + 1) * 128],
                                 xn2bf[:, ic * 512:(ic + 1) * 512]) for ic in range(IC)])
                B.pe.inc(ins)
            pe_qk = B.pe.n

            @blk.scalar
            def _(sc, pe_qk=pe_qk):
                sc.wait_ge(B.pe.h, pe_qk)
                _li = sc.activation(qkf[:], ps_a[:], AF.Silu, bias=B.bqk_c[:, 0:1])
                B.se.inc(_li)
            se_qk.append(B.se.n)
            se_k = B.se.n

            @blk.vector
            def _(v, t_=t_, se_k=se_k):
                v.wait_ge(B.se.h, se_k)
                _li = v.tensor_scalar(B.k1[:, t_ * 512:(t_ + 1) * 512], qkf[:],
                                B.osw[:, 1:2], B.osb[:, 1:2], ALU.mult, ALU.add)
                B.ve.inc(_li)

            for sub in range(4):
                vi = t_ * 4 + sub
                vt = vtmp[vi % 2]
                vs = vst[vi % 2]
                for hc in range(4):
                    def ev(blk2, bank, pe_t, vt=vt, hc=hc, vi=vi):
                        @blk2.vector
                        def _(v):
                            v.wait_ge(B.pe.h, pe_t)
                            if hc == 0:
                                v.wait_ge(B.se.h, se_v[vi])
                            _li = v.tensor_add(vt[:, hc * 512:(hc + 1) * 512], bank[:],
                                         bhv_bc[:, hc * 512:(hc + 1) * 512])
                            B.ve.inc(_li)
                        return ("ve", B.ve.n)
                    vpipe.step(
                        blk,
                        [(xn2bf[:, ic * 512 + sub * 128: ic * 512 + sub * 128 + 128],
                          whv[:, ic * HID + hc * 512: ic * HID + hc * 512 + 512])
                         for ic in range(IC)],
                        ev,
                        mm_waits=[dg_whv])
                ve_v4 = B.ve.n

                @blk.scalar
                def _(sc, vi=vi, vt=vt, vs=vs, ve_v4=ve_v4):
                    sc.wait_ge(B.ve.h, ve_v4)
                    if vi >= 2:
                        sc.wait_ge(*ds_v[vi])
                    _li = sc.activation(vs[:], vt[:], AF.Silu)
                    B.se.inc(_li)
                se_v.append(B.se.n)
                se_s = B.se.n

                @blk.sync
                def _(sync, vi=vi, vs=vs, se_s=se_s):
                    sync.wait_ge(B.se.h, se_s)
                    r0 = vi * 128
                    s = lf_v.nxt()
                    s.inc(sync.dma_start(out=B.d("v1d")[r0:r0 + 128, :], in_=vs[:]), 16)
                ds_v.append(lf_v.sems[vi % 2].tok())
            pe_mm_end.append(B.pe.n)



def _ph_attn1(B, blk, phz):
    nc = B.nc
    _reset_attn1_hists()
    mbase = phz * HALF
    with ExitStack() as ph:
        sb = lambda n, s, d=F32: ph.enter_context(nc.sbuf_tensor(n, s, d))
        pps = lambda n, s, d=F32: ph.enter_context(nc.psum_tensor(n, s, d))
        vh = sb(f"a1_vh{phz}", [128, 32 * HID], BF16)
        prob = sb(f"a1_prob{phz}", [128, HALF], BF16)
        probT = sb(f"a1_probT{phz}", [128, HALF], BF16)
        rtmp = [sb(f"a1_rt{phz}{i}", [128, 512]) for i in range(2)]
        aw = [sb(f"a1_aw{phz}{i}", [128, HID]) for i in range(2)]
        ps_sc = [pps(f"a1_psc{phz}{i}", [128, 512], F32) for i in range(2)]
        ps_av = [pps(f"a1_pav{phz}{i}", [128, 512], F32) for i in range(4)]
        ps_t = [pps(f"a1_pst{phz}{i}", [128, 1024], BF16) for i in range(2)]

        a1_v = DS(B, f"a1_v{phz}", "sync")

        @blk.sync
        def _(sync):
            s = a1_v.nxt()
            s.inc(sync.dma_start(out=vh[:].rearrange("p (s d) -> p s d", s=32),
                                 in_=B.d("v1d")[mbase:mbase + HALF, :].rearrange("(s p) d -> p s d", p=128)), 16)
        ds_v = a1_v.sems[0].tok()

        a1_f = DS(B, f"a1_f{phz}", "sync" if phz == 0 else "gp", parity=2)
        se_sq_q = [0]
        pe_tr_q = [0]
        pe_av_q = [0]
        done_q = [0]
        ve_relu = []
        se_sq_h = []
        tr_ve_all = []

        for q in range(QT):
            for mc in range(8):
                pp = ps_sc[mc % 2]

                @blk.tensor
                def _(t, q=q, mc=mc, pp=pp):
                    if q == 0 and mc == 0:
                        t.wait_ge(*ds_v)
                    kk = len(ve_relu)
                    if kk >= 2:
                        t.wait_ge(B.ve.h, ve_relu[kk - 2])
                    _li = t.matmul(pp[:], B.q1[:, q * 128:(q + 1) * 128],
                             B.k1[:, mbase + mc * 512: mbase + mc * 512 + 512],
                             start=True, stop=True)
                    B.pe.inc(_li)
                pe_mc = B.pe.n

                @blk.vector
                def _(v, q=q, mc=mc, pp=pp, pe_mc=pe_mc):
                    v.wait_ge(B.pe.h, pe_mc)
                    kk = len(ve_relu)
                    if kk >= 2:
                        v.wait_ge(B.se.h, se_sq_h[kk - 2])
                    _li = v.tensor_scalar_max(rtmp[mc % 2][:], pp[:], 0.0)
                    B.ve.inc(_li)
                ve_relu.append(B.ve.n)
                ve_r = B.ve.n

                @blk.scalar
                def _(sc, q=q, mc=mc, ve_r=ve_r):
                    sc.wait_ge(B.ve.h, ve_r)
                    if mc == 0:
                        sc.wait_ge(B.pe.h, pe_tr_q[q])
                    _li = sc.activation(prob[:, mc * 512:(mc + 1) * 512], rtmp[mc % 2][:],
                                  AF.Square, scale=1.0 / S)
                    B.se.inc(_li)
                se_sq_h.append(B.se.n)
            se_sq_q.append(B.se.n)
            se_p = B.se.n

            tr_ve = []
            for mc in range(32):
                @blk.tensor
                def _(t, q=q, mc=mc, se_p=se_p):
                    if mc == 0:
                        t.wait_ge(B.se.h, se_p)
                    if len(tr_ve) >= 2:
                        t.wait_ge(B.ve.h, tr_ve[-2])
                    _li = t.transpose(ps_t[mc % 2][:, 0:128], prob[:, mc * 128:(mc + 1) * 128],
                                B.ident_bf[:])
                    B.pe.inc(_li)
                pe_tr = B.pe.n

                @blk.vector
                def _(v, mc=mc, pe_tr=pe_tr, q=q):
                    v.wait_ge(B.pe.h, pe_tr)
                    if mc == 0:
                        v.wait_ge(B.pe.h, pe_av_q[q])
                    _li = v.tensor_copy(probT[:, mc * 128:(mc + 1) * 128], ps_t[mc % 2][:, 0:128])
                    B.ve.inc(_li)
                tr_ve.append(B.ve.n)
            pe_tr_q.append(B.pe.n)
            ve_tr = B.ve.n

            @blk.tensor
            def _(t, q=q, ve_tr=ve_tr):
                t.wait_ge(B.ve.h, ve_tr)
                t.wait_ge(B.ve.h, done_q[q] if phz == 0 else 0)
                if phz == 1:
                    t.wait_ge(B.ve.h, ve_aw_q[q])
                for mc in range(32):
                    for oh in range(4):
                        _li = t.matmul(ps_av[oh][:],
                                 probT[:, mc * 128:(mc + 1) * 128],
                                 vh[:, mc * HID + oh * 512: mc * HID + oh * 512 + 512],
                                 start=(mc == 0), stop=(mc == 31))
                B.pe.inc(_li)
            pe_av_q.append(B.pe.n)
            pe_av = B.pe.n

            a = aw[q % 2]

            @blk.vector
            def _(v, q=q, a=a, pe_av=pe_av):
                v.wait_ge(B.pe.h, pe_av)
                if q >= 2:
                    v.wait_ge(*flush_q[q - 1])
                for oh in range(4):
                    _li = v.tensor_copy(a[:, oh * 512:(oh + 1) * 512], ps_av[oh][:])
                B.ve.inc(_li)
            ve_aw_q.append(B.ve.n)
            ve_a = B.ve.n

            if phz == 0:
                @blk.sync
                def _(sync, q=q, a=a, ve_a=ve_a):
                    sync.wait_ge(B.ve.h, ve_a)
                    s = a1_f.nxt()
                    s.inc(sync.dma_start(out=B.d("acc1")[q * 128:(q + 1) * 128, :], in_=a[:]), 16)
                flush_q.append(a1_f.sems[q % 2].tok())
                done_q.append(ve_a)
            else:
                @blk.gpsimd
                def _(gp, q=q, a=a, ve_a=ve_a):
                    gp.wait_ge(B.ve.h, ve_a)
                    s = a1_f.nxt()
                    s.inc(gp.dma_start(out=B.d("acc1")[q * 128:(q + 1) * 128, :], in_=a[:],
                                       accum_op=ALU.add), 16)
                flush_q.append(a1_f.sems[q % 2].tok())
                done_q.append(ve_a)


ve_aw_q = [0]
flush_q = [0]


def _reset_attn1_hists():
    ve_aw_q.clear(); ve_aw_q.append(0)
    flush_q.clear(); flush_q.append(0)



def _ph_yproj(B, blk, x1own_src):
    nc, I = B.nc, B.inp
    with ExitStack() as ph:
        sb = lambda n, s, d=F32: ph.enter_context(nc.sbuf_tensor(n, s, d))
        pps = lambda n, s, d=F32: ph.enter_context(nc.psum_tensor(n, s, d))
        wop = sb("yp_wo", [128, 16 * D], BF16)
        bo_bc = sb("yp_bo", [128, D], F32)
        bo_row = sb("yp_borow", [1, D], F32)
        accb = [sb(f"yp_ac{i}", [128, HID]) for i in range(2)]
        gateb = [sb(f"yp_gt{i}", [128, HID], BF16) for i in range(2)]
        gatef = sb("yp_gtf", [128, HID], F32)
        x1b = [sb(f"yp_x{i}", [128, D]) for i in range(2)]
        moeb = [sb(f"yp_m{i}", [128, D]) for i in range(2)]
        zbf = sb("yp_z", [128, HID], BF16)
        zT = sb("yp_zT", [128, HID], BF16)
        sw = sb("yp_sw", [128, D])
        ps_y = [pps(f"yp_py{i}", [128, 512], F32) for i in range(2)]
        ps_t = [pps(f"yp_pt{i}", [128, 1024], BF16) for i in range(2)]

        g_wo = DS(B, "g_wo", "gp")

        @blk.gpsimd
        def _(gp):
            s = g_wo.nxt()
            for hc in range(16):
                s.inc(gp.dma_start(out=wop[:, hc * D:(hc + 1) * D],
                                   in_=I["W_out"][hc * 128:(hc + 1) * 128, :]), 16)
        dg_wo = g_wo.sems[0].tok()

        y_bo = DS(B, "y_bo", "sync")

        @blk.sync
        def _(sync):
            s = y_bo.nxt()
            s.inc(sync.dma_start(out=bo_row[:], in_=I["b_out"][:].rearrange("(a d) -> a d", a=1)), 16)
        ds_bo = y_bo.sems[0].tok()
        bc = Pipe(B, ps_y)
        for c in range(2):
            def ev(blk2, bank, pe_t, c=c):
                @blk2.vector
                def _(v):
                    v.wait_ge(B.pe.h, pe_t)
                    _li = v.tensor_copy(bo_bc[:, c * 512:(c + 1) * 512], bank[:])
                    B.ve.inc(_li)
                return ("ve", B.ve.n)
            bc.step(blk, [(B.ones_f[:], bo_row[:, c * 512:(c + 1) * 512])], ev,
                    mm_waits=[ds_bo])

        y_in = DS(B, "y_in", "sync", parity=2)
        y_out = DS(B, "y_out", "sync", parity=2)
        ds_in = []
        ve_done = [0, 0]
        ds_out = [None, None]
        ve_z_hist = [0]
        tr_pe, tr_ve = [], []
        ypipe = Pipe(B, ps_y)
        ypipe.free.extend(bc.free[-2:])

        for q in range(QT):
            p = q % 2

            @blk.sync
            def _(sync, q=q, p=p):
                if q >= 2:
                    sync.wait_ge(B.ve.h, ve_done[q])
                s = y_in.nxt()
                s.inc(sync.dma_start(out=accb[p][:], in_=B.d("acc1")[q * 128:(q + 1) * 128, :]), 16)
                s.inc(sync.dma_start(out=gateb[p][:], in_=B.d("gate1")[q * 128:(q + 1) * 128, :]), 16)
                s.inc(sync.dma_start(out=x1b[p][:].rearrange("p (a d) -> p a d", a=1),
                                     in_=x1own_src[q * 128:(q + 1) * 128, :].rearrange("(s p) d -> p s d", p=128)), 16)
                s.inc(sync.dma_start(out=moeb[p][:].rearrange("p (a d) -> p a d", a=1),
                                     in_=B.d("moe1")[q * 128:(q + 1) * 128, :].rearrange("(s p) d -> p s d", p=128)), 16)
            ds_in.append(y_in.sems[q % 2].tok())

            @blk.scalar
            def _(sc, q=q, p=p):
                sc.wait_ge(*ds_in[q])
                sc.wait_ge(B.ve.h, ve_z_hist[-1])
                _li = sc.copy(gatef[:], gateb[p][:])
                B.se.inc(_li)
            se_g = B.se.n

            @blk.vector
            def _(v, q=q, p=p, se_g=se_g):
                v.wait_ge(B.se.h, se_g)
                if tr_pe:
                    v.wait_ge(B.pe.h, tr_pe[-1])
                _li = v.tensor_mul(zbf[:], accb[p][:], gatef[:])
                B.ve.inc(_li)
            ve_z = B.ve.n
            ve_z_hist.append(ve_z)

            for hc in range(16):
                @blk.tensor
                def _(t, hc=hc, ve_z=ve_z):
                    if hc == 0:
                        t.wait_ge(B.ve.h, ve_z)
                    if len(tr_pe) >= 2:
                        t.wait_ge(B.ve.h, tr_ve[-2])
                    _li = t.transpose(ps_t[hc % 2][:, 0:128], zbf[:, hc * 128:(hc + 1) * 128],
                                B.ident_bf[:])
                    B.pe.inc(_li)
                tr_pe.append(B.pe.n)

                @blk.vector
                def _(v, hc=hc, q=q):
                    v.wait_ge(B.pe.h, tr_pe[-1])
                    _li = v.tensor_copy(zT[:, hc * 128:(hc + 1) * 128], ps_t[hc % 2][:, 0:128])
                    B.ve.inc(_li)
                tr_ve.append(B.ve.n)
            ve_zT = B.ve.n

            for oh in range(2):
                def ev(blk2, bank, pe_t, q=q, p=p, oh=oh, ds_prev=ds_out[-1]):
                    @blk2.vector
                    def _(v):
                        v.wait_ge(B.pe.h, pe_t)
                        if oh == 0 and ds_prev is not None:
                            v.wait_ge(*ds_prev)
                        _li = v.tensor_add(sw[:, oh * 512:(oh + 1) * 512], bank[:],
                                     bo_bc[:, oh * 512:(oh + 1) * 512])
                        B.ve.inc(_li)
                    return ("ve", B.ve.n)
                ypipe.step(
                    blk,
                    [(zT[:, hc * 128:(hc + 1) * 128],
                      wop[:, hc * D + oh * 512: hc * D + oh * 512 + 512])
                     for hc in range(16)],
                    ev,
                    mm_waits=[(B.ve.h, ve_zT), dg_wo])
            ve_y = B.ve.n

            @blk.scalar
            def _(sc, q=q, p=p, ve_y=ve_y):
                sc.wait_ge(B.ve.h, ve_y)
                _li = sc.activation(moeb[p][:], moeb[p][:], AF.Silu)
                B.se.inc(_li)
            se_m = B.se.n

            @blk.vector
            def _(v, q=q, p=p, se_m=se_m):
                v.wait_ge(B.se.h, se_m)
                if q >= 2:
                    v.wait_ge(*ds_out[q])
                v.tensor_mul(sw[:], sw[:], moeb[p][:])
                _li = v.tensor_add(sw[:], sw[:], x1b[p][:])
                B.ve.inc(_li)
            ve_done.append(B.ve.n)
            ve_f = B.ve.n

            @blk.sync
            def _(sync, q=q, ve_f=ve_f):
                sync.wait_ge(B.ve.h, ve_f)
                s = y_out.nxt()
                s.inc(sync.dma_start(out=B.out_own[q * 128:(q + 1) * 128, :], in_=sw[:]), 16)
            ds_out.append(y_out.sems[q % 2].tok())



MODE = "unfused"

_L0_KEYS = dict(
    W_spin="L0_attn_W_spin", b_spin="L0_attn_b_spin", W_global="L0_attn_W_global",
    W_J="L0_attn_W_J", beta="L0_attn_beta", r0w="L0_ffn_router_w",
    w1="L0_ffn_w1", b1="L0_ffn_b1", w2="L0_ffn_w2", b2="L0_ffn_b2")
_L1_KEYS = dict(
    ln_scale="L1_attn_ln_scale", ln_bias="L1_attn_ln_bias",
    W_hidden="L1_attn_W_hidden", b_hidden="L1_attn_b_hidden",
    W_qk="L1_attn_W_qk", b_qk="L1_attn_b_qk", os_weight="L1_attn_os_weight",
    os_bias="L1_attn_os_bias", W_out="L1_attn_W_out", b_out="L1_attn_b_out",
    r1w="L1_ffn_router_w", l1w1="L1_ffn_w1", l1w2="L1_ffn_w2",
    sgu="L1_ffn_shared_gate_up", sdown="L1_ffn_shared_down",
    sgw="L1_ffn_shared_gate_w")


def _consts():
    return {
        "c_ident_bf": np.eye(128, dtype=ml_dtypes.bfloat16),
        "c_ident_f32": np.eye(128, dtype=np.float32),
        "c_ones_f32": np.ones((1, 128), np.float32),
        "c_eps": np.full((128, 1), EPS, np.float32),
    }


def _f32(a):
    return np.ascontiguousarray(np.asarray(a), dtype=np.float32)


def kernel(**inp):
    x = _f32(inp["L0_attn_x"])
    consts = _consts()
    wl0 = {k: _f32(inp[v]) for k, v in _L0_KEYS.items()}
    wl1 = {k: _f32(inp[v]) for k, v in _L1_KEYS.items()}

    def core_x(c):
        b, h = c // 2, c % 2
        return (np.ascontiguousarray(x[b, h * HALF:(h + 1) * HALF]),
                np.ascontiguousarray(x[b, (1 - h) * HALF:(1 - h + 1) * HALF]))

    if MODE == "unfused":
        ncA = build_program("l0")
        mapsA = []
        for c in range(8):
            xo, xp = core_x(c)
            mapsA.append(dict(consts, **wl0, xo=xo, xp=xp))
        resA = run_bass_kernel_spmd(ncA, mapsA, core_ids=list(range(8)))
        x1o = [np.asarray(resA.results[c]["x1o"]) for c in range(8)]

        ncB = build_program("l1")
        mapsB = []
        for c in range(8):
            prt = c + 1 if c % 2 == 0 else c - 1
            x1f = np.ascontiguousarray(np.concatenate([x1o[c], x1o[prt]], axis=0))
            mapsB.append(dict(consts, **wl1, x1f_in=x1f, x1own_in=x1o[c]))
        resB = run_bass_kernel_spmd(ncB, mapsB, core_ids=list(range(8)))
        outs = [np.asarray(resB.results[c]["oout"]) for c in range(8)]
    else:
        ncF = build_program("fused")
        maps = []
        for c in range(8):
            xo, xp = core_x(c)
            maps.append(dict(consts, **wl0, **wl1, xo=xo, xp=xp))
        res = run_bass_kernel_spmd(ncF, maps, core_ids=list(range(8)))
        outs = [np.asarray(res.results[c]["oout"]) for c in range(8)]

    out = np.empty((4, S, D), np.float32)
    for c in range(8):
        out[c // 2, (c % 2) * HALF:(c % 2 + 1) * HALF] = outs[c]
    return out
